# version 09
# baseline (speedup 1.0000x reference)
import numpy as np
import ml_dtypes
from contextlib import ExitStack
from types import SimpleNamespace

import concourse.bass as bass
import concourse.mybir as mybir
from concourse.bass_utils import run_bass_kernel_spmd

F32 = mybir.dt.float32
BF16 = mybir.dt.bfloat16
AF = mybir.ActivationFunctionType
ALU = mybir.AluOpType
AX = mybir.AxisListType

NCORE = 8
D = 1024
TPC = 2048
NTB = 4
TBW = 512
NDC = 8
NEXP = 8
HID = 4096
NHC = 32
CAP = 5120.0
EPS_LN = 1e-5
EPS_ATT = 1e-6
EPS_FEAT = 1e-6
PNORM = 1.0 / (64.0 ** 0.25)

ENGINES = ("sync", "gpsimd", "tensor", "vector", "scalar")

_TAPS = ()


class K:

    def __init__(self, nc, stk):
        self.nc = nc
        self.stk = stk
        self.seq = []
        self.ctrs = []
        self.veS = self.ctr("veS")
        self.scS = self.ctr("scS")
        self.mmS = self.ctr("mmS")
        self.bw = [None] * 8
        self.br = [[] for _ in range(8)]

    def ctr(self, name):
        h = self.stk.enter_context(self.nc.semaphore(name))
        c = SimpleNamespace(h=h, n=0, name=name)
        self.ctrs.append(c)
        return c

    def emit(self, eng, fn, waits=(), incs=()):
        seen = {}
        for c, v in waits:
            assert 0 <= v <= c.n, f"wait {c.name}>={v} but only {c.n} emitted"
            if v > 0:
                prev = seen.get(id(c))
                seen[id(c)] = (c.h, v if prev is None else max(v, prev[1]))
        ws = list(seen.values())
        rincs = []
        for c, k in incs:
            c.n += k
            rincs.append((c.h, k))
        self.seq.append((eng, ws, fn, rincs))

    def v(self, fn, waits=()):
        w = list(waits)
        if self.veS.n:
            w.append((self.veS, self.veS.n))
        self.emit("vector", fn, w, [(self.veS, 1)])
        return (self.veS, self.veS.n)

    def s(self, fn, waits=()):
        w = list(waits)
        if self.scS.n:
            w.append((self.scS, self.scS.n))
        self.emit("scalar", fn, w, [(self.scS, 1)])
        return (self.scS, self.scS.n)

    def dma(self, eng, fn, c, waits=()):
        self.emit(eng, fn, waits, [(c, 16)])
        return (c, c.n)

    def g(self, fn, waits=(), incs=()):
        self.emit("gpsimd", fn, waits, incs)

    def mm(self, bank, fn, waits=()):
        w = list(waits) + self.br[bank]
        if self.bw[bank] is not None:
            w.append(self.bw[bank])
        self.emit("tensor", fn, w, [(self.mmS, 1)])
        tok = (self.mmS, self.mmS.n)
        self.bw[bank] = tok
        self.br[bank] = []
        return tok

    def rd(self, eng, bank, fn, waits=()):
        w = list(waits)
        if self.bw[bank] is not None:
            w.append(self.bw[bank])
        tok = self.v(fn, w) if eng == "vector" else self.s(fn, w)
        self.br[bank].append(tok)
        return tok

    def snap(self):
        return [(c, c.n) for c in self.ctrs]

    def barrier_all(self, exclude=()):
        ex = set(id(c) for c in exclude)
        sn = [(c, v) for c, v in self.snap() if id(c) not in ex]
        for e in ENGINES:
            self.emit(e, None, waits=[(c, v) for c, v in sn if v > 0])
        self.bw = [None] * 8
        self.br = [[] for _ in range(8)]

    def flush(self):
        with self.nc.Block(no_gpsimd_drain=True) as block:
            for en in ENGINES:
                def run(engine, en=en):
                    for eng, ws, fn, incs in self.seq:
                        if eng != en:
                            continue
                        for h, v in ws:
                            engine.wait_ge(h, v)
                        inst = fn(engine) if fn is not None else None
                        for h, kk in incs:
                            inst.then_inc(h, kk)
                getattr(block, en)(run)
        self.seq = []


def build_program():
    nc = bass.Bass("TRN2", target_bir_lowering=False)

    def inp(name, shape, dt=F32):
        return nc.dram_tensor(name, list(shape), dt, kind="ExternalInput").ap()

    xfm = inp("xfm", (NDC, 128, TPC))
    wqT = inp("wqT", (D, D))
    wkT = inp("wkT", (D, D))
    wvT = inp("wvT", (D, D))
    woT = inp("woT", (D, D))
    bqkvo = inp("bqkvo", (128, 32))
    bvrow = inp("bvrow", (1, D))
    ptT = inp("ptT", (64, 64))
    gw = inp("gw", (D, NEXP))
    cfc = inp("cfc", (NEXP, D, HID))
    fcbi = inp("fcbi", (NEXP, 128, 32))
    cpj = inp("cpj", (NEXP, HID, D))
    pjbi = inp("pjbi", (NEXP, 128, 8))
    cmself = inp("cmself", (8, 8))
    cmprev = inp("cmprev", (8, 8))

    identf_t = nc.inline_tensor(np.eye(128, dtype=np.float32), "identc")
    selm = np.zeros((16, 2048), np.float32)
    for r_ in range(16):
        selm[r_, r_ * 128:(r_ + 1) * 128] = 1.0
    selm_t = nc.inline_tensor(selm, "selmc")

    def intern(name, shape, dt=F32):
        kind = "ExternalOutput" if name in _TAPS else "Internal"
        return nc.dram_tensor(name, list(shape), dt, kind=kind).ap()

    x1d = intern("x1d", (NDC, 128, TPC))
    xn2d = intern("xn2d", (NDC, 128, TPC), BF16)
    cfc_bf = intern("cfc_bf", (NEXP, D, HID), BF16)
    cpj_bf = intern("cpj_bf", (NEXP, HID, D), BF16)
    kvin = intern("kvin", (128, 520))
    kvout = intern("kvout", (128, 520))
    ccin = intern("ccin", (16, 8))
    ccout = intern("ccout", (16, 8))
    ysfm = nc.dram_tensor("ysfm", [NDC, 128, TPC], F32, kind="ExternalOutput").ap()


    stk = ExitStack()
    with stk:
        E = stk.enter_context
        P = K(nc, stk)

        qC = P.ctr("qC")
        qX = [P.ctr(f"qX{i}") for i in range(4)]
        qWT = P.ctr("qWT")
        qKV = P.ctr("qKV")
        qST = [P.ctr(f"qST{i}") for i in range(2)]
        qN = [P.ctr(f"qN{i}") for i in range(2)]
        qCC = P.ctr("qCC")
        qFW = P.ctr("qFW")
        qFX = [P.ctr(f"qFX{i}") for i in range(2)]
        qYP = [P.ctr(f"qYP{i}") for i in range(2)]
        qFS = [P.ctr(f"qFS{i}") for i in range(2)]
        qOUT = P.ctr("qOUT")
        ccS = P.ctr("ccS")
        qTAP = P.ctr("qTAP")

        def tap(name, buf, rows, cols, dt=F32):
            if name not in _TAPS:
                return
            t = nc.dram_tensor(name, [rows, cols], dt, kind="ExternalOutput").ap()
            P.barrier_all()
            P.dma("sync", lambda e: e.dma_start(out=t, in_=buf[0:rows, 0:cols]), qTAP)
            P.barrier_all()

        onesC = E(nc.sbuf_tensor("onesC", [128, 128], F32))
        selS = E(nc.sbuf_tensor("selS", [16, 2048], F32))
        w_sb = E(nc.sbuf_tensor("w_sb", [16, 2048], F32))
        fcbs = E(nc.sbuf_tensor("fcbs", [128, 256], F32))
        pjbs = E(nc.sbuf_tensor("pjbs", [128, 64], F32))
        pb = [E(nc.psum_tensor(f"pb{i}", [128, 512], F32)) for i in range(8)]

        P.dma("sync", lambda e: e.dma_start(out=selS[:], in_=selm_t.ap()), qC)
        for e_ in range(NEXP):
            P.dma("sync", lambda e, e_=e_: e.dma_start(out=fcbs[:, e_ * 32:(e_ + 1) * 32], in_=fcbi[e_]), qC)
            P.dma("sync", lambda e, e_=e_: e.dma_start(out=pjbs[:, e_ * 8:(e_ + 1) * 8], in_=pjbi[e_]), qC)
        P.v(lambda e: e.memset(w_sb[:], 0.0))
        P.v(lambda e: e.memset(onesC[:], 1.0))

        def ln_stats(src, xbuf, sqbuf, arena):
            slot_use = [None] * 4
            sq_use = [None, None]
            for dc in range(NDC):
                for tb in range(NTB):
                    i = dc * NTB + tb
                    sl = i % 4
                    w = [slot_use[sl]] if slot_use[sl] else []
                    ld = P.dma("sync", lambda e, dc=dc, tb=tb, sl=sl: e.dma_start(
                        out=xbuf[:, sl * 512:(sl + 1) * 512],
                        in_=src[dc, :, tb * TBW:(tb + 1) * TBW]), qX[sl], w)
                    sq = P.s(lambda e, sl=sl: e.activation(
                        sqbuf[:, (sl % 2) * 512:(sl % 2 + 1) * 512],
                        xbuf[:, sl * 512:(sl + 1) * 512], AF.Square),
                        [ld] + ([sq_use[sl % 2]] if sq_use[sl % 2] else []))
                    P.mm(tb, lambda e, tb=tb, sl=sl, dc=dc: e.matmul(
                        pb[tb][0:33, :], onesC[:, 0:33], xbuf[:, sl * 512:(sl + 1) * 512],
                        start=(dc == 0), stop=(dc == NDC - 1)), [ld])
                    tok = P.mm(4 + tb, lambda e, tb=tb, sl=sl, dc=dc: e.matmul(
                        pb[4 + tb][0:33, :], onesC[:, 0:33],
                        sqbuf[:, (sl % 2) * 512:(sl % 2 + 1) * 512],
                        start=(dc == 0), stop=(dc == NDC - 1)), [sq])
                    slot_use[sl] = tok
                    sq_use[sl % 2] = tok
            stats_done = (P.mmS, P.mmS.n)
            for tb in range(NTB):
                cl = slice(tb * TBW, (tb + 1) * TBW)
                P.rd("vector", tb, lambda e, tb=tb, cl=cl: e.tensor_scalar(
                    arena[0:1, cl], pb[tb][0:1, :], 1.0 / D, None, ALU.mult), [stats_done])
                P.rd("vector", tb, lambda e, tb=tb, cl=cl: e.tensor_scalar(
                    sqbuf[32:33, 0:512], pb[tb][32:33, :], 1.0 / D, None, ALU.mult))
                P.v(lambda e: e.tensor_tensor(sqbuf[32:33, 0:512], sqbuf[32:33, 0:512],
                                              sqbuf[32:33, 0:512], ALU.mult))
                P.rd("vector", 4 + tb, lambda e, tb=tb: e.tensor_scalar(
                    sqbuf[32:33, 512:1024], pb[4 + tb][32:33, :], 1.0 / D, EPS_LN,
                    ALU.mult, ALU.add))
                P.v(lambda e: e.tensor_tensor(sqbuf[32:33, 512:1024], sqbuf[32:33, 512:1024],
                                              sqbuf[32:33, 0:512], ALU.subtract))
                P.s(lambda e: e.activation(sqbuf[32:33, 0:512], sqbuf[32:33, 512:1024],
                                           AF.Sqrt), [(P.veS, P.veS.n)])
                P.v(lambda e, cl=cl: e.reciprocal(arena[32:33, cl], sqbuf[32:33, 0:512]),
                    [(P.scS, P.scS.n)])

        with ExitStack() as s1:
            EA = s1.enter_context
            xn = EA(nc.sbuf_tensor("xn", [128, NDC * TPC], BF16))
            qf = EA(nc.sbuf_tensor("qf", [128, NDC * TPC], BF16))
            kf = EA(nc.sbuf_tensor("kf", [128, 16 * 1024], BF16))
            vt = EA(nc.sbuf_tensor("vt", [128, 16 * 1024], BF16))
            qk = EA(nc.sbuf_tensor("qk", [128, TPC], BF16))
            xst = EA(nc.sbuf_tensor("xst", [128, 4 * 512], F32))
            wts = EA(nc.sbuf_tensor("wts", [128, 8 * 1024], BF16))
            scr = EA(nc.sbuf_tensor("scr", [128, 3 * 512], F32))
            rbsb = EA(nc.sbuf_tensor("rbsb", [128, 512], F32))
            stg = EA(nc.sbuf_tensor("stg", [128, 2 * 512], F32))
            kvpack = EA(nc.sbuf_tensor("kvpack", [128, 520], F32))
            kv_bf = EA(nc.sbuf_tensor("kv_bf", [128, 520], BF16))
            blkP = EA(nc.sbuf_tensor("blkP", [128, 128], BF16))
            bqs = EA(nc.sbuf_tensor("bqs", [128, 32], F32))
            bvb = EA(nc.sbuf_tensor("bvb", [128, D], F32))
            ptsb = EA(nc.sbuf_tensor("ptsb", [128, 64], F32))
            ones_bf = EA(nc.sbuf_tensor("ones_bf", [128, 1], BF16))
            arena = EA(nc.sbuf_tensor("arena", [48, 2048], F32))
            P.v(lambda e: e.memset(arena[:], 0.0))

            P.dma("sync", lambda e: e.dma_start(out=bqs[:], in_=bqkvo), qC)
            P.dma("sync", lambda e: e.dma_start(out=ptsb[0:64, :], in_=ptT), qC)
            P.dma("sync", lambda e: e.dma_start(out=ptsb[64:128, :], in_=ptT), qC)
            P.dma("sync", lambda e: e.dma_start(out=bvb[0:1, :], in_=bvrow), qC)
            cload = (qC, qC.n)
            P.v(lambda e: e.memset(ones_bf[:], 1.0))
            mz = P.v(lambda e: e.memset(blkP[:], 0.0))
            P.s(lambda e: e.activation(blkP[0:64, 0:64], ptsb[0:64, :], AF.Copy, scale=PNORM),
                [cload, mz])
            blkP_rdy = P.s(lambda e: e.activation(blkP[64:128, 64:128], ptsb[64:128, :],
                                                  AF.Copy, scale=PNORM), [cload])
            vz = (P.veS, P.veS.n)
            P.mm(6, lambda e: e.matmul(pb[6][:], onesC[0:1, :], bvb[0:1, 0:512],
                                       start=True, stop=True), [cload, vz])
            P.mm(7, lambda e: e.matmul(pb[7][:], onesC[0:1, :], bvb[0:1, 512:1024],
                                       start=True, stop=True), [cload, vz])
            P.rd("vector", 6, lambda e: e.tensor_copy(bvb[:, 0:512], pb[6][:]))
            bvb_rdy = P.rd("vector", 7, lambda e: e.tensor_copy(bvb[:, 512:1024], pb[7][:]))

            def load_wt(src, waits):
                last = None
                for dc in range(NDC):
                    last = P.dma("gpsimd", lambda e, dc=dc, src=src: e.dma_start(
                        out=wts[:, dc * 1024:(dc + 1) * 1024],
                        in_=src[dc * 128:(dc + 1) * 128, :]), qWT,
                        waits if dc == 0 else [])
                return last

            wq_rdy = load_wt(wqT, [])

            P.barrier_all()
            ln_stats(xfm, xst, scr, arena)
            P.barrier_all()
            tap("t_ar", arena, 48, 2048, F32)

            slot_use = [None] * 4
            for tb in range(NTB):
                bm, br_ = 4 + (tb % 2) * 2, 5 + (tb % 2) * 2
                P.mm(bm, lambda e, tb=tb, bm=bm: e.matmul(
                    pb[bm][:], onesC[0:1, :], arena[0:1, tb * TBW:(tb + 1) * TBW],
                    start=True, stop=True))
                P.mm(br_, lambda e, tb=tb, br_=br_: e.matmul(
                    pb[br_][:], onesC[32:33, :], arena[32:33, tb * TBW:(tb + 1) * TBW],
                    start=True, stop=True))
                for dc in range(NDC):
                    i = tb * NDC + dc
                    sl = i % 4
                    w = [slot_use[sl]] if slot_use[sl] else []
                    ld = P.dma("sync", lambda e, dc=dc, tb=tb, sl=sl: e.dma_start(
                        out=xst[:, sl * 512:(sl + 1) * 512],
                        in_=xfm[dc, :, tb * TBW:(tb + 1) * TBW]), qX[sl], w)
                    P.rd("vector", bm, lambda e, sl=sl, bm=bm: e.tensor_tensor(
                        scr[:, (sl % 2) * 512:(sl % 2 + 1) * 512],
                        xst[:, sl * 512:(sl + 1) * 512], pb[bm][:], ALU.subtract), [ld])
                    tok = P.rd("vector", br_, lambda e, sl=sl, dc=dc, tb=tb, br_=br_: e.tensor_tensor(
                        xn[:, dc * TPC + tb * TBW: dc * TPC + (tb + 1) * TBW],
                        scr[:, (sl % 2) * 512:(sl % 2 + 1) * 512], pb[br_][:], ALU.mult))
                    slot_use[sl] = tok
            P.barrier_all()
            tap("t_xn", xn, 128, NDC * TPC, BF16)

            qk_guard = None

            def proj(which, w_rdy, bias_off):
                nonlocal qk_guard
                for oc in range(NDC):
                    banks = [0, 1, 2, 3] if oc % 2 == 0 else [4, 5, 6, 7]
                    for tb in range(NTB):
                        bk = banks[tb]
                        def mmg2(e, oc=oc, tb=tb, bk=bk):
                            last = None
                            for dc in range(NDC):
                                last = e.matmul(
                                    pb[bk][:],
                                    wts[:, dc * 1024 + oc * 128: dc * 1024 + (oc + 1) * 128],
                                    xn[:, dc * TPC + tb * TBW: dc * TPC + (tb + 1) * TBW],
                                    start=(dc == 0), stop=(dc == NDC - 1))
                            return last
                        P.mm(bk, mmg2, [w_rdy])
                    for tb in range(NTB):
                        bk = banks[tb]
                        w = [qk_guard] if qk_guard else []
                        P.rd("scalar", bk, lambda e, oc=oc, tb=tb, bk=bk: e.add(
                            qk[:, tb * TBW:(tb + 1) * TBW], pb[bk][:],
                            bqs[:, bias_off + oc: bias_off + oc + 1]), w)
                    if which == "q":
                        for tb in range(NTB):
                            bk = banks[tb]
                            tok = P.mm(bk, lambda e, tb=tb, bk=bk: e.matmul(
                                pb[bk][:], blkP[:], qk[:, tb * TBW:(tb + 1) * TBW],
                                start=True, stop=True), [blkP_rdy])
                            P.rd("vector", bk, lambda e, oc=oc, tb=tb, bk=bk: e.tensor_scalar(
                                qf[:, oc * TPC + tb * TBW: oc * TPC + (tb + 1) * TBW],
                                pb[bk][:], 0.0, EPS_FEAT, ALU.max, ALU.add))
                        qk_guard = tok
                    else:
                        for tb4 in range(NTB):
                            bk = banks[tb4]
                            def mmk(e, oc=oc, tb4=tb4, bk=bk):
                                last = None
                                for j in range(4):
                                    tc = tb4 * 4 + j
                                    last = e.matmul(
                                        pb[bk][:, j * 128:(j + 1) * 128],
                                        qk[:, tc * 128:(tc + 1) * 128], blkP[:],
                                        start=True, stop=True)
                                return last
                            tok = P.mm(bk, mmk, [blkP_rdy])
                            def evk(e, oc=oc, tb4=tb4, bk=bk):
                                last = None
                                for j in range(4):
                                    tc = tb4 * 4 + j
                                    last = e.tensor_scalar(
                                        kf[:, tc * 1024 + oc * 128: tc * 1024 + (oc + 1) * 128],
                                        pb[bk][:, j * 128:(j + 1) * 128], 0.0, EPS_FEAT,
                                        ALU.max, ALU.add)
                                return last
                            P.rd("vector", bk, evk)
                        qk_guard = tok

            proj("q", wq_rdy, 0)
            wk_rdy = load_wt(wkT, [(P.mmS, P.mmS.n)])
            proj("k", wk_rdy, 8)
            wv_rdy = load_wt(wvT, [(P.mmS, P.mmS.n)])

            for tc in range(16):
                for half in range(2):
                    bk = 4 * (tc % 2) + half
                    def mmv(e, tc=tc, half=half, bk=bk):
                        last = None
                        for dc in range(NDC):
                            last = e.matmul(
                                pb[bk][:],
                                xn[:, dc * TPC + tc * 128: dc * TPC + (tc + 1) * 128],
                                wts[:, dc * 1024 + half * 512: dc * 1024 + (half + 1) * 512],
                                start=(dc == 0), stop=(dc == NDC - 1))
                        return last
                    P.mm(bk, mmv, [wv_rdy])
                    P.rd("vector", bk, lambda e, tc=tc, half=half, bk=bk: e.scalar_tensor_tensor(
                        vt[:, tc * 1024 + half * 512: tc * 1024 + (half + 1) * 512],
                        pb[bk][:], 1.0, bvb[:, half * 512:(half + 1) * 512],
                        ALU.mult, ALU.add), [bvb_rdy])
            wo_rdy = load_wt(woT, [(P.mmS, P.mmS.n)])
            P.barrier_all()
            tap("t_qf", qf, 128, NDC * TPC, BF16)
            tap("t_kf", kf, 128, 16 * 1024, BF16)
            tap("t_vt", vt, 128, 16 * 1024, BF16)

            for hp in range(NDC):
                bk = hp % 2
                for par in range(2):
                    base = par * 64
                    h_ = hp * 2 + par
                    def mmkv(e, h_=h_, base=base, bk=bk):
                        last = None
                        for tc in range(16):
                            last = e.matmul(
                                pb[bk][base:base + 64, 0:64],
                                kf[:, tc * 1024 + h_ * 64: tc * 1024 + (h_ + 1) * 64],
                                vt[:, tc * 1024 + h_ * 64: tc * 1024 + (h_ + 1) * 64],
                                start=(tc == 0), stop=(tc == 15))
                        return last
                    P.mm(bk, mmkv)
                def mmks(e, hp=hp):
                    last = None
                    for tc in range(16):
                        last = e.matmul(
                            pb[2][:, hp:hp + 1],
                            kf[:, tc * 1024 + hp * 128: tc * 1024 + (hp + 1) * 128],
                            ones_bf[:], start=(tc == 0), stop=(tc == 15))
                    return last
                P.mm(2, mmks)
                P.rd("vector", bk, lambda e, hp=hp, bk=bk: e.tensor_copy(
                    kvpack[:, hp * 65: hp * 65 + 64], pb[bk][:, 0:64]))
            P.rd("vector", 2, lambda e: e.tensor_copy(kvpack[:, 64:520:65], pb[2][:, 0:8]))
            kv_done = (P.veS, P.veS.n)
            P.dma("gpsimd", lambda e: e.dma_start(out=kvin, in_=kvpack[:]), qKV, [kv_done])
            P.g(lambda e: e.collective_compute(
                "AllReduce", ALU.add, replica_groups=[[0, 1], [2, 3], [4, 5], [6, 7]],
                ins=[kvin.opt()], outs=[kvout.opt()]), [(qKV, qKV.n)], [(ccS, 1)])
            kvld = P.dma("gpsimd", lambda e: e.dma_start(out=kvpack[:], in_=kvout), qKV, [(ccS, 1)])
            kvbf_rdy = P.v(lambda e: e.tensor_copy(kv_bf[:], kvpack[:]), [kvld])
            P.barrier_all()
            tap("t_kv", kvpack, 128, 520, F32)

            rden_guard = None
            rbsE_guard = None
            rbsO_guard = None
            for hp in range(NDC):
                for tb in range(NTB):
                    i = hp * NTB + tb
                    bkE = i % 2
                    bkO = 2 + i % 2
                    bkD = 4 + i % 2
                    c0 = hp * TPC + tb * TBW
                    P.mm(bkE, lambda e, hp=hp, c0=c0, bkE=bkE: e.matmul(
                        pb[bkE][0:65, :], kv_bf[0:64, hp * 65:(hp + 1) * 65],
                        qf[0:64, c0:c0 + TBW], start=True, stop=True))
                    P.mm(bkO, lambda e, hp=hp, c0=c0, bkO=bkO: e.matmul(
                        pb[bkO][64:128, :], kv_bf[64:128, hp * 65: hp * 65 + 64],
                        qf[64:128, c0:c0 + TBW], start=True, stop=True))
                    P.mm(bkD, lambda e, hp=hp, c0=c0, bkD=bkD: e.matmul(
                        pb[bkD][64:65, :], kv_bf[64:128, hp * 65 + 64: hp * 65 + 65],
                        qf[64:128, c0:c0 + TBW], start=True, stop=True))
                    wrd = [rden_guard] if rden_guard else []
                    P.rd("vector", bkE, lambda e, bkE=bkE: e.tensor_scalar(
                        scr[64:65, 0:512], pb[bkE][64:65, :], EPS_ATT, None, ALU.add), wrd)
                    P.v(lambda e: e.reciprocal(scr[64:65, 0:512], scr[64:65, 0:512]))
                    P.rd("vector", bkD, lambda e, bkD=bkD: e.tensor_scalar(
                        scr[64:65, 512:1024], pb[bkD][64:65, :], EPS_ATT, None, ALU.add))
                    rcp = P.v(lambda e: e.reciprocal(scr[64:65, 512:1024], scr[64:65, 512:1024]))
                    bE = P.mm(6, lambda e: e.matmul(
                        pb[6][0:64, :], onesC[64:65, 0:64], scr[64:65, 0:512],
                        start=True, stop=True), [rcp])
                    bO = P.mm(7, lambda e: e.matmul(
                        pb[7][64:128, :], onesC[64:65, 0:64], scr[64:65, 512:1024],
                        start=True, stop=True), [rcp])
                    rden_guard = bO
                    rbsE = P.rd("scalar", 6, lambda e: e.copy(rbsb[0:64, :], pb[6][0:64, :]),
                                [rbsE_guard] if rbsE_guard else [])
                    rbsO = P.rd("scalar", 7, lambda e: e.copy(rbsb[64:128, :], pb[7][64:128, :]),
                                [rbsO_guard] if rbsO_guard else [])
                    rbsE_guard = P.rd("vector", bkE, lambda e, c0=c0, bkE=bkE: e.tensor_tensor(
                        qf[0:64, c0:c0 + TBW], pb[bkE][0:64, :], rbsb[0:64, :],
                        ALU.mult), [rbsE])
                    rbsO_guard = P.rd("vector", bkO, lambda e, c0=c0, bkO=bkO: e.tensor_tensor(
                        qf[64:128, c0:c0 + TBW], pb[bkO][64:128, :], rbsb[64:128, :],
                        ALU.mult), [rbsO])
            P.barrier_all()
            tap("t_att", qf, 128, NDC * TPC, BF16)

            slot_use = [None] * 4
            store_use = [None, None]
            for oc in range(NDC):
                banks = [0, 1, 2, 3] if oc % 2 == 0 else [4, 5, 6, 7]
                for tb in range(NTB):
                    bk = banks[tb]
                    def mmo(e, oc=oc, tb=tb, bk=bk):
                        last = None
                        for dc in range(NDC):
                            last = e.matmul(
                                pb[bk][:],
                                wts[:, dc * 1024 + oc * 128: dc * 1024 + (oc + 1) * 128],
                                qf[:, dc * TPC + tb * TBW: dc * TPC + (tb + 1) * TBW],
                                start=(dc == 0), stop=(dc == NDC - 1))
                        return last
                    P.mm(bk, mmo, [wo_rdy])
                for tb in range(NTB):
                    bk = banks[tb]
                    i = oc * NTB + tb
                    sl = i % 4
                    ssl = i % 2
                    w = [slot_use[sl]] if slot_use[sl] else []
                    ld = P.dma("sync", lambda e, oc=oc, tb=tb, sl=sl: e.dma_start(
                        out=xst[:, sl * 512:(sl + 1) * 512],
                        in_=xfm[oc, :, tb * TBW:(tb + 1) * TBW]), qX[sl], w)
                    wst = [ld] + ([store_use[ssl]] if store_use[ssl] else [])
                    ev = P.rd("vector", bk, lambda e, oc=oc, sl=sl, ssl=ssl, bk=bk: e.scalar_tensor_tensor(
                        stg[:, ssl * 512:(ssl + 1) * 512], pb[bk][:],
                        bqs[:, 24 + oc: 25 + oc], xst[:, sl * 512:(sl + 1) * 512],
                        ALU.add, ALU.add), wst)
                    slot_use[sl] = ev
                    P.dma("sync", lambda e, oc=oc, tb=tb, ssl=ssl: e.dma_start(
                        out=x1d[oc, :, tb * TBW:(tb + 1) * TBW],
                        in_=stg[:, ssl * 512:(ssl + 1) * 512]), qST[ssl], [ev])
                    store_use[ssl] = P.dma("sync", lambda e, oc=oc, tb=tb, ssl=ssl: e.dma_start(
                        out=ysfm[oc, :, tb * TBW:(tb + 1) * TBW],
                        in_=stg[:, ssl * 512:(ssl + 1) * 512]), qST[ssl], [])
            P.barrier_all()
            P.flush()

        s23 = ExitStack()
        w1s = s23.enter_context(nc.sbuf_tensor("w1s", [128, NDC * HID], BF16))
        qPC = [P.ctr(f"qPC{i}") for i in range(NEXP)]
        qFW2 = P.ctr("qFW2")
        with s23, ExitStack() as s2:
            EG = s2.enter_context
            xst2 = EG(nc.sbuf_tensor("xst2", [128, 4 * 512], F32))
            scr2 = EG(nc.sbuf_tensor("scr2", [128, 2 * 512], F32))
            xn2fs = EG(nc.sbuf_tensor("xn2fs", [128, 2 * 512], F32))
            xn2bst = EG(nc.sbuf_tensor("xn2bst", [128, 2 * 512], BF16))
            gsb = EG(nc.sbuf_tensor("gsb", [128, 64], F32))
            lg_sb = EG(nc.sbuf_tensor("lg_sb", [16, 2048], F32))
            mp1 = EG(nc.sbuf_tensor("mp1", [8, 2048], F32))
            mp2 = EG(nc.sbuf_tensor("mp2", [8, 2048], F32))
            lc1 = EG(nc.sbuf_tensor("lc1", [8, 2048], F32))
            lc2 = EG(nc.sbuf_tensor("lc2", [8, 2048], F32))
            tpl = EG(nc.sbuf_tensor("tpl", [8, 2048], F32))
            zz = EG(nc.sbuf_tensor("zz", [8, 2048], F32))
            lgt = EG(nc.sbuf_tensor("lgt", [128, 16], F32))
            rt = EG(nc.sbuf_tensor("rt", [128, 96], F32))
            cnt = EG(nc.sbuf_tensor("cnt", [8, 32], F32))
            cmsb = EG(nc.sbuf_tensor("cmsb", [8, 16], F32))
            arena = EG(nc.sbuf_tensor("arena2", [48, 2048], F32))
            identf = EG(nc.sbuf_tensor("identf", [128, 128], F32))

            P.v(lambda e: e.memset(arena[:], 0.0))
            P.dma("sync", lambda e: e.dma_start(out=identf[:], in_=identf_t.ap()), qC)
            P.dma("sync", lambda e: e.dma_start(
                out=gsb[:, 0:64].rearrange("p (a e) -> p a e", e=8),
                in_=gw.rearrange("(a p) e -> p a e", p=128)), qC)
            P.dma("sync", lambda e: e.dma_start(out=cmsb[:, 0:8], in_=cmself), qC)
            P.dma("sync", lambda e: e.dma_start(out=cmsb[:, 8:16], in_=cmprev), qC)

            ln_stats(x1d, xst2, scr2, arena)
            P.barrier_all()

            slot_use = [None] * 4
            nstore_use = [None, None]
            lg_guard = [None, None]
            for tb in range(NTB):
                bm, br_, bl = tb % 2, 2 + tb % 2, 4 + tb % 2
                P.mm(bm, lambda e, tb=tb, bm=bm: e.matmul(
                    pb[bm][:], onesC[0:1, :], arena[0:1, tb * TBW:(tb + 1) * TBW],
                    start=True, stop=True))
                P.mm(br_, lambda e, tb=tb, br_=br_: e.matmul(
                    pb[br_][:], onesC[32:33, :], arena[32:33, tb * TBW:(tb + 1) * TBW],
                    start=True, stop=True))
                for dc in range(NDC):
                    i = tb * NDC + dc
                    sl = i % 4
                    xsl = dc % 2
                    w = [slot_use[sl]] if slot_use[sl] else []
                    ld = P.dma("sync", lambda e, dc=dc, tb=tb, sl=sl: e.dma_start(
                        out=xst2[:, sl * 512:(sl + 1) * 512],
                        in_=x1d[dc, :, tb * TBW:(tb + 1) * TBW]), qX[sl], w)
                    P.rd("vector", bm, lambda e, sl=sl, bm=bm, xsl=xsl: e.tensor_tensor(
                        scr2[:, xsl * 512:(xsl + 1) * 512],
                        xst2[:, sl * 512:(sl + 1) * 512], pb[bm][:], ALU.subtract), [ld])
                    wg = [lg_guard[xsl]] if lg_guard[xsl] else []
                    mtok = P.rd("vector", br_, lambda e, xsl=xsl, br_=br_: e.tensor_tensor(
                        xn2fs[:, xsl * 512:(xsl + 1) * 512],
                        scr2[:, xsl * 512:(xsl + 1) * 512], pb[br_][:], ALU.mult), wg)
                    slot_use[sl] = mtok
                    ct = P.s(lambda e, xsl=xsl: e.copy(
                        xn2bst[:, xsl * 512:(xsl + 1) * 512],
                        xn2fs[:, xsl * 512:(xsl + 1) * 512]),
                        [mtok] + ([nstore_use[xsl]] if nstore_use[xsl] else []))
                    nstore_use[xsl] = P.dma("sync", lambda e, dc=dc, tb=tb, xsl=xsl: e.dma_start(
                        out=xn2d[dc, :, tb * TBW:(tb + 1) * TBW],
                        in_=xn2bst[:, xsl * 512:(xsl + 1) * 512]), qN[xsl], [ct])
                    lg_guard[xsl] = P.mm(bl, lambda e, dc=dc, xsl=xsl, bl=bl: e.matmul(
                        pb[bl][0:8, :], gsb[:, dc * 8:(dc + 1) * 8],
                        xn2fs[:, xsl * 512:(xsl + 1) * 512],
                        start=(dc == 0), stop=(dc == NDC - 1)), [mtok, (qC, qC.n)])
                P.rd("vector", bl, lambda e, tb=tb, bl=bl: e.tensor_copy(
                    lg_sb[0:8, tb * TBW:(tb + 1) * TBW], pb[bl][0:8, :]))
            P.barrier_all()

            P.v(lambda e: e.memset(zz[:], 0.0))
            P.v(lambda e: e.memset(rt[:], 0.0))
            for tt in range(16):
                P.mm(6, lambda e, tt=tt: e.transpose(
                    pb[6][:, 0:8], lg_sb[0:8, tt * 128:(tt + 1) * 128], identf[0:8, 0:8]))
                P.rd("vector", 6, lambda e: e.tensor_copy(lgt[:, 0:8], pb[6][:, 0:8]))
                P.v(lambda e: e.tensor_reduce(rt[:, 64:65], lgt[:, 0:8], AX.X, ALU.max))
                P.v(lambda e: e.tensor_scalar(rt[:, 0:8], lgt[:, 0:8], rt[:, 64:65], None, ALU.is_ge))
                P.v(lambda e: e.tensor_tensor_scan(rt[:, 8:16], rt[:, 0:8], rt[:, 80:88],
                                                   0.0, ALU.add, ALU.add))
                P.v(lambda e: e.tensor_scalar(rt[:, 16:24], rt[:, 8:16], 1.0, None, ALU.is_le))
                P.v(lambda e: e.tensor_tensor(rt[:, 24:32], rt[:, 0:8], rt[:, 16:24], ALU.mult))
                P.v(lambda e: e.scalar_tensor_tensor(rt[:, 32:40], rt[:, 24:32], -1e30,
                                                     lgt[:, 0:8], ALU.mult, ALU.add))
                P.v(lambda e: e.tensor_reduce(rt[:, 65:66], rt[:, 32:40], AX.X, ALU.max))
                P.v(lambda e: e.tensor_tensor(rt[:, 66:67], rt[:, 64:65], rt[:, 65:66], ALU.subtract))
                P.s(lambda e: e.activation(rt[:, 67:68], rt[:, 66:67], AF.Sigmoid), [(P.veS, P.veS.n)])
                sg = P.s(lambda e: e.activation(rt[:, 68:69], rt[:, 66:67], AF.Sigmoid, scale=-1.0))
                P.v(lambda e: e.tensor_scalar(rt[:, 40:48], rt[:, 32:40], rt[:, 65:66], None, ALU.is_ge))
                P.v(lambda e: e.tensor_tensor_scan(rt[:, 48:56], rt[:, 40:48], rt[:, 80:88],
                                                   0.0, ALU.add, ALU.add))
                P.v(lambda e: e.tensor_scalar(rt[:, 16:24], rt[:, 48:56], 1.0, None, ALU.is_le))
                P.v(lambda e: e.tensor_tensor(rt[:, 56:64], rt[:, 40:48], rt[:, 16:24], ALU.mult))
                P.v(lambda e: e.tensor_scalar(rt[:, 0:8], rt[:, 24:32], rt[:, 67:68], None, ALU.mult), [sg])
                P.v(lambda e: e.tensor_scalar(rt[:, 8:16], rt[:, 56:64], rt[:, 68:69], None, ALU.mult))
                P.mm(6, lambda e: e.transpose(pb[6][0:8, 0:128], rt[:, 0:8], identf[:]),
                     [(P.veS, P.veS.n)])
                P.rd("vector", 6, lambda e, tt=tt: e.tensor_copy(
                    mp1[0:8, tt * 128:(tt + 1) * 128], pb[6][0:8, 0:128]))
                P.mm(7, lambda e: e.transpose(pb[7][0:8, 0:128], rt[:, 8:16], identf[:]),
                     [(P.veS, P.veS.n)])
                P.rd("vector", 7, lambda e, tt=tt: e.tensor_copy(
                    mp2[0:8, tt * 128:(tt + 1) * 128], pb[7][0:8, 0:128]))

            P.v(lambda e: e.tensor_scalar(tpl[:], mp1[:], 0.0, None, ALU.is_gt))
            P.v(lambda e: e.tensor_tensor_scan(lc1[:], tpl[:], zz[:], 0.0, ALU.add, ALU.add))
            P.v(lambda e: e.tensor_scalar(tpl[:], mp2[:], 0.0, None, ALU.is_gt))
            P.v(lambda e: e.tensor_tensor_scan(lc2[:], tpl[:], zz[:], 0.0, ALU.add, ALU.add))
            P.v(lambda e: e.tensor_scalar(cnt[:, 0:8], cmsb[:, 0:8], lc1[:, 2047:2048], None, ALU.mult))
            ctok = P.v(lambda e: e.tensor_scalar(cnt[:, 8:16], cmsb[:, 0:8], lc2[:, 2047:2048],
                                                 None, ALU.mult))
            P.dma("gpsimd", lambda e: e.dma_start(out=ccin[0:8, :], in_=cnt[:, 0:8]), qCC, [ctok])
            P.dma("gpsimd", lambda e: e.dma_start(out=ccin[8:16, :], in_=cnt[:, 8:16]), qCC)
            P.g(lambda e: e.collective_compute(
                "AllReduce", ALU.add, replica_groups=[list(range(NCORE))],
                ins=[ccin.opt()], outs=[ccout.opt()]), [(qCC, qCC.n), (ccS, 1)], [(ccS, 1)])
            l1 = P.dma("gpsimd", lambda e: e.dma_start(out=cnt[:, 16:24], in_=ccout[0:8, :]), qCC, [(ccS, 2)])
            l2 = P.dma("gpsimd", lambda e: e.dma_start(out=cnt[:, 24:32], in_=ccout[8:16, :]), qCC)
            P.v(lambda e: e.tensor_tensor(rt[0:8, 0:8], cnt[:, 16:24], cmsb[:, 8:16], ALU.mult), [l1, l2])
            P.v(lambda e: e.tensor_reduce(rt[0:8, 64:65], rt[0:8, 0:8], AX.X, ALU.add))
            P.v(lambda e: e.tensor_reduce(rt[0:8, 65:66], cnt[:, 16:24], AX.X, ALU.add))
            P.v(lambda e: e.tensor_tensor(rt[0:8, 8:16], cnt[:, 24:32], cmsb[:, 8:16], ALU.mult))
            P.v(lambda e: e.tensor_reduce(rt[0:8, 66:67], rt[0:8, 8:16], AX.X, ALU.add))
            P.v(lambda e: e.tensor_tensor(rt[0:8, 66:67], rt[0:8, 66:67], rt[0:8, 65:66], ALU.add))
            P.v(lambda e: e.tensor_scalar(tpl[:], lc1[:], rt[0:8, 64:65], None, ALU.add))
            P.v(lambda e: e.tensor_scalar(lc1[:], tpl[:], CAP, None, ALU.is_le))
            P.v(lambda e: e.tensor_tensor(mp1[:], lc1[:], mp1[:], ALU.mult))
            P.v(lambda e: e.tensor_scalar(tpl[:], lc2[:], rt[0:8, 66:67], None, ALU.add))
            P.v(lambda e: e.tensor_scalar(lc2[:], tpl[:], CAP, None, ALU.is_le))
            P.v(lambda e: e.tensor_tensor(mp2[:], lc2[:], mp2[:], ALU.mult))
            P.v(lambda e: e.tensor_tensor(w_sb[0:8, :], mp1[:], mp2[:], ALU.add))
            pc_tok = []
            for ex in range(NEXP):
                P.dma("gpsimd", lambda e, ex=ex: e.dma_start(out=cfc_bf[ex], in_=cfc[ex]), qPC[ex])
                pc_tok.append(P.dma("gpsimd", lambda e, ex=ex: e.dma_start(
                    out=cpj_bf[ex], in_=cpj[ex]), qPC[ex]))
            w1_rdy0 = None
            for dc in range(NDC):
                w1_rdy0 = P.dma("sync", lambda e, dc=dc: e.dma_start(
                    out=w1s[:, dc * HID:(dc + 1) * HID],
                    in_=cfc_bf[0, dc * 128:(dc + 1) * 128, :]), qFW,
                    [pc_tok[0]] if dc == 0 else [])
            P.barrier_all(exclude=tuple(qPC) + (qFW, qFW2))
            P.flush()
            s2.close()

            s3 = ExitStack()
            EH = s3.enter_context
            w2s = EH(nc.sbuf_tensor("w2s", [128, NHC * 1024], BF16))
            hbf = EH(nc.sbuf_tensor("hbf", [128, NHC * 512], BF16))
            xts = EH(nc.sbuf_tensor("xts", [128, 2 * NDC * 512], BF16))
            wbs = EH(nc.sbuf_tensor("wbs", [128, 2 * 512], F32))
            yst = EH(nc.sbuf_tensor("yst", [128, 2 * 512], F32))

            xts_use = [None, None]
            yst_use = [None, None]
            wbs_use = [None, None]
            p1_done = None
            p2_done = None
            hbf_guard = None
            w1_next = w1_rdy0

            def load_w2(ex, waits):
                tok = None
                for hc in range(NHC):
                    tok = P.dma("sync", lambda e, ex=ex, hc=hc: e.dma_start(
                        out=w2s[:, hc * 1024:(hc + 1) * 1024],
                        in_=cpj_bf[ex, hc * 128:(hc + 1) * 128, :]), qFW2,
                        waits if hc == 0 else [])
                return tok

            def load_w1(ex, waits):
                tok = None
                for dc in range(NDC):
                    tok = P.dma("sync", lambda e, ex=ex, dc=dc: e.dma_start(
                        out=w1s[:, dc * HID:(dc + 1) * HID],
                        in_=cfc_bf[ex, dc * 128:(dc + 1) * 128, :]), qFW,
                        waits if dc == 0 else [])
                return tok

            for ex in range(NEXP):
                w1_rdy = w1_next
                w2_rdy = load_w2(ex, [pc_tok[ex]] + ([p2_done] if p2_done else []))
                for ts in range(NTB):
                    sl = (ex * NTB + ts) % 2
                    ldx = None
                    for dc in range(NDC):
                        ldx = P.dma("sync", lambda e, ts=ts, dc=dc, sl=sl: e.dma_start(
                            out=xts[:, sl * 4096 + dc * 512: sl * 4096 + (dc + 1) * 512],
                            in_=xn2d[dc, :, ts * TBW:(ts + 1) * TBW]), qFX[sl],
                            ([xts_use[sl]] if xts_use[sl] and dc == 0 else []))
                    wb = P.mm(4 + sl, lambda e, ex=ex, ts=ts, sl=sl: e.matmul(
                        pb[4 + sl][:], selS[:, ex * 128:(ex + 1) * 128],
                        w_sb[:, ts * TBW:(ts + 1) * TBW], start=True, stop=True),
                        [wbs_use[sl]] if wbs_use[sl] else [])
                    wbr = P.rd("scalar", 4 + sl, lambda e, sl=sl: e.copy(
                        wbs[:, sl * 512:(sl + 1) * 512], pb[4 + sl][:]))
                    for hc in range(NHC):
                        bk = hc % 2
                        def mm1(e, hc=hc, sl=sl, bk=bk):
                            last = None
                            for dc in range(NDC):
                                last = e.matmul(
                                    pb[bk][:],
                                    w1s[:, dc * HID + hc * 128: dc * HID + (hc + 1) * 128],
                                    xts[:, sl * 4096 + dc * 512: sl * 4096 + (dc + 1) * 512],
                                    start=(dc == 0), stop=(dc == NDC - 1))
                            return last
                        P.mm(bk, mm1, [w1_rdy, ldx])
                        P.rd("scalar", bk, lambda e, ex=ex, hc=hc, bk=bk: e.activation(
                            hbf[:, hc * 512:(hc + 1) * 512], pb[bk][:], AF.Gelu,
                            bias=fcbs[:, ex * 32 + hc: ex * 32 + hc + 1]),
                            [hbf_guard] if hbf_guard else [])
                    gelu_all = (P.scS, P.scS.n)
                    if ts == NTB - 1:
                        p1_done = (P.mmS, P.mmS.n)
                        if ex + 1 < NEXP:
                            w1_next = load_w1(ex + 1, [p1_done, pc_tok[ex + 1]])
                    for dc in range(NDC):
                        bk = 2 + dc % 2
                        def mm2(e, dc=dc, bk=bk):
                            last = None
                            for hc in range(NHC):
                                last = e.matmul(
                                    pb[bk][:],
                                    w2s[:, hc * 1024 + dc * 128: hc * 1024 + (dc + 1) * 128],
                                    hbf[:, hc * 512:(hc + 1) * 512],
                                    start=(hc == 0), stop=(hc == NHC - 1))
                            return last
                        P.mm(bk, mm2, [gelu_all, w2_rdy])
                        ysl = dc % 2
                        ev = P.rd("vector", bk, lambda e, ex=ex, dc=dc, ysl=ysl, sl=sl, bk=bk: e.scalar_tensor_tensor(
                            yst[:, ysl * 512:(ysl + 1) * 512], pb[bk][:],
                            pjbs[:, ex * 8 + dc: ex * 8 + dc + 1],
                            wbs[:, sl * 512:(sl + 1) * 512], ALU.add, ALU.mult),
                            [wbr] + ([yst_use[ysl]] if yst_use[ysl] else []))
                        yst_use[ysl] = P.dma("gpsimd", lambda e, ex=ex, dc=dc, ts=ts, ysl=ysl: e.dma_start(
                            out=ysfm[dc, :, ts * TBW:(ts + 1) * TBW],
                            in_=yst[:, ysl * 512:(ysl + 1) * 512],
                            accum_op=ALU.add), qYP[ysl], [ev])
                    hbf_guard = (P.mmS, P.mmS.n)
                    xts_use[sl] = (P.mmS, P.mmS.n)
                    wbs_use[sl] = (P.veS, P.veS.n)
                p2_done = (P.mmS, P.mmS.n)
            P.barrier_all()
            s3.close()
            P.flush()

        P.emit("gpsimd", None, waits=[(qYP[0], qYP[0].n), (qYP[1], qYP[1].n),
                                      (qST[0], qST[0].n), (qST[1], qST[1].n)])
        P.flush()

    return nc


def kernel(L0_attn_x, L0_attn_Wq, L0_attn_bq, L0_attn_Wk, L0_attn_bk,
           L0_attn_Wv, L0_attn_bv, L0_attn_Wo, L0_attn_bo, L0_attn_P,
           L0_ffn_gate_w, L0_ffn_c_fc, L0_ffn_fc_bias, L0_ffn_c_proj,
           L0_ffn_proj_bias):
    x = np.ascontiguousarray(np.asarray(L0_attn_x, dtype=np.float32))
    B, N, D_ = x.shape
    X2 = x.reshape(B * N, D_)

    c = np.ascontiguousarray
    shared = dict(
        wqT=c(np.asarray(L0_attn_Wq, np.float32).T),
        wkT=c(np.asarray(L0_attn_Wk, np.float32).T),
        wvT=c(np.asarray(L0_attn_Wv, np.float32).T),
        woT=c(np.asarray(L0_attn_Wo, np.float32).T),
        bqkvo=c(np.concatenate(
            [np.asarray(b, np.float32).reshape(8, 128).T
             for b in (L0_attn_bq, L0_attn_bk, L0_attn_bv, L0_attn_bo)], axis=1)),
        bvrow=c(np.asarray(L0_attn_bv, np.float32).reshape(1, D_)),
        ptT=c(np.asarray(L0_attn_P, np.float32).T),
        gw=c(np.asarray(L0_ffn_gate_w, np.float32)),
        cfc=c(np.asarray(L0_ffn_c_fc, np.float32)),
        fcbi=c(np.stack([np.asarray(L0_ffn_fc_bias, np.float32)[e].reshape(32, 128).T
                         for e in range(NEXP)])),
        cpj=c(np.asarray(L0_ffn_c_proj, np.float32)),
        pjbi=c(np.stack([np.asarray(L0_ffn_proj_bias, np.float32)[e].reshape(8, 128).T
                         for e in range(NEXP)])),
    )

    in_maps = []
    for core in range(NCORE):
        sl = X2[core * TPC:(core + 1) * TPC]
        xfm_ = c(sl.T.reshape(NDC, 128, TPC))
        cms_ = np.zeros((8, 8), np.float32)
        cms_[:, core] = 1.0
        cmp_ = np.zeros((8, 8), np.float32)
        cmp_[:, :core] = 1.0
        m = dict(shared)
        m.update(xfm=xfm_, cmself=cms_, cmprev=cmp_)
        in_maps.append(m)

    nc = build_program()
    res = run_bass_kernel_spmd(nc, in_maps, core_ids=list(range(NCORE)))

    out = np.empty((B * N, D_), np.float32)
    for core in range(NCORE):
        yf = np.asarray(res.results[core]["ysfm"])
        out[core * TPC:(core + 1) * TPC] = yf.reshape(D_, TPC).T
    return out.reshape(B, N, D_)


# version 10
# speedup vs baseline: 1.0112x; 1.0112x over previous
import numpy as np
import ml_dtypes
from contextlib import ExitStack
from types import SimpleNamespace

import concourse.bass as bass
import concourse.mybir as mybir
from concourse.bass_utils import run_bass_kernel_spmd

F32 = mybir.dt.float32
BF16 = mybir.dt.bfloat16
AF = mybir.ActivationFunctionType
ALU = mybir.AluOpType
AX = mybir.AxisListType

NCORE = 8
D = 1024
TPC = 2048
NTB = 4
TBW = 512
NDC = 8
NEXP = 8
HID = 4096
NHC = 32
CAP = 5120.0
EPS_LN = 1e-5
EPS_ATT = 1e-6
EPS_FEAT = 1e-6
PNORM = 1.0 / (64.0 ** 0.25)

ENGINES = ("sync", "gpsimd", "tensor", "vector", "scalar")

_TAPS = ()


class K:

    def __init__(self, nc, stk):
        self.nc = nc
        self.stk = stk
        self.seq = []
        self.ctrs = []
        self.veS = self.ctr("veS")
        self.scS = self.ctr("scS")
        self.mmS = self.ctr("mmS")
        self.bw = [None] * 8
        self.br = [[] for _ in range(8)]

    def ctr(self, name):
        h = self.stk.enter_context(self.nc.semaphore(name))
        c = SimpleNamespace(h=h, n=0, name=name)
        self.ctrs.append(c)
        return c

    def emit(self, eng, fn, waits=(), incs=()):
        seen = {}
        for c, v in waits:
            assert 0 <= v <= c.n, f"wait {c.name}>={v} but only {c.n} emitted"
            if v > 0:
                prev = seen.get(id(c))
                seen[id(c)] = (c.h, v if prev is None else max(v, prev[1]))
        ws = list(seen.values())
        rincs = []
        for c, k in incs:
            c.n += k
            rincs.append((c.h, k))
        self.seq.append((eng, ws, fn, rincs))

    def v(self, fn, waits=()):
        w = list(waits)
        if self.veS.n:
            w.append((self.veS, self.veS.n))
        self.emit("vector", fn, w, [(self.veS, 1)])
        return (self.veS, self.veS.n)

    def s(self, fn, waits=()):
        w = list(waits)
        if self.scS.n:
            w.append((self.scS, self.scS.n))
        self.emit("scalar", fn, w, [(self.scS, 1)])
        return (self.scS, self.scS.n)

    def dma(self, eng, fn, c, waits=()):
        self.emit(eng, fn, waits, [(c, 16)])
        return (c, c.n)

    def g(self, fn, waits=(), incs=()):
        self.emit("gpsimd", fn, waits, incs)

    def mm(self, bank, fn, waits=()):
        w = list(waits) + self.br[bank]
        if self.bw[bank] is not None:
            w.append(self.bw[bank])
        self.emit("tensor", fn, w, [(self.mmS, 1)])
        tok = (self.mmS, self.mmS.n)
        self.bw[bank] = tok
        self.br[bank] = []
        return tok

    def rd(self, eng, bank, fn, waits=()):
        w = list(waits)
        if self.bw[bank] is not None:
            w.append(self.bw[bank])
        tok = self.v(fn, w) if eng == "vector" else self.s(fn, w)
        self.br[bank].append(tok)
        return tok

    def snap(self):
        return [(c, c.n) for c in self.ctrs]

    def barrier_all(self, exclude=()):
        ex = set(id(c) for c in exclude)
        sn = [(c, v) for c, v in self.snap() if id(c) not in ex]
        for e in ENGINES:
            self.emit(e, None, waits=[(c, v) for c, v in sn if v > 0])
        self.bw = [None] * 8
        self.br = [[] for _ in range(8)]

    def flush(self):
        with self.nc.Block(no_gpsimd_drain=True) as block:
            for en in ENGINES:
                def run(engine, en=en):
                    for eng, ws, fn, incs in self.seq:
                        if eng != en:
                            continue
                        for h, v in ws:
                            engine.wait_ge(h, v)
                        inst = fn(engine) if fn is not None else None
                        for h, kk in incs:
                            inst.then_inc(h, kk)
                getattr(block, en)(run)
        self.seq = []


def build_program():
    nc = bass.Bass("TRN2", target_bir_lowering=False)

    def inp(name, shape, dt=F32):
        return nc.dram_tensor(name, list(shape), dt, kind="ExternalInput").ap()

    xfm = inp("xfm", (NDC, NTB, 128, TBW))
    wqT = inp("wqT", (D, D))
    wkT = inp("wkT", (D, D))
    wvT = inp("wvT", (D, D))
    woT = inp("woT", (D, D))
    bqkvo = inp("bqkvo", (128, 32))
    bvrow = inp("bvrow", (1, D))
    ptT = inp("ptT", (64, 64))
    gw = inp("gw", (D, NEXP))
    cfc = inp("cfc", (NEXP, D, HID))
    fcbi = inp("fcbi", (NEXP, 128, 32))
    cpj = inp("cpj", (NEXP, HID, D))
    pjbi = inp("pjbi", (NEXP, 128, 8))
    cmself = inp("cmself", (8, 8))
    cmprev = inp("cmprev", (8, 8))

    identf_t = nc.inline_tensor(np.eye(128, dtype=np.float32), "identc")
    selm = np.zeros((16, 2048), np.float32)
    for r_ in range(16):
        selm[r_, r_ * 128:(r_ + 1) * 128] = 1.0
    selm_t = nc.inline_tensor(selm, "selmc")

    def intern(name, shape, dt=F32):
        kind = "ExternalOutput" if name in _TAPS else "Internal"
        return nc.dram_tensor(name, list(shape), dt, kind=kind).ap()

    x1d = intern("x1d", (NDC, NTB, 128, TBW))
    xn2d = intern("xn2d", (NDC, NTB, 128, TBW), BF16)
    cfc_bf = intern("cfc_bf", (NEXP, D, HID), BF16)
    cpj_bf = intern("cpj_bf", (NEXP, HID, D), BF16)
    kvin = intern("kvin", (128, 520))
    kvout = intern("kvout", (128, 520))
    ccin = intern("ccin", (16, 8))
    ccout = intern("ccout", (16, 8))
    ysfm = nc.dram_tensor("ysfm", [NDC, NTB, 128, TBW], F32, kind="ExternalOutput").ap()


    stk = ExitStack()
    with stk:
        E = stk.enter_context
        P = K(nc, stk)

        qC = P.ctr("qC")
        qX = [P.ctr(f"qX{i}") for i in range(4)]
        qWT = P.ctr("qWT")
        qKV = P.ctr("qKV")
        qST = [P.ctr(f"qST{i}") for i in range(2)]
        qN = [P.ctr(f"qN{i}") for i in range(2)]
        qCC = P.ctr("qCC")
        qFW = P.ctr("qFW")
        qFX = [P.ctr(f"qFX{i}") for i in range(2)]
        qYP = [P.ctr(f"qYP{i}") for i in range(2)]
        qFS = [P.ctr(f"qFS{i}") for i in range(2)]
        qOUT = P.ctr("qOUT")
        ccS = P.ctr("ccS")
        qTAP = P.ctr("qTAP")

        def tap(name, buf, rows, cols, dt=F32):
            if name not in _TAPS:
                return
            t = nc.dram_tensor(name, [rows, cols], dt, kind="ExternalOutput").ap()
            P.barrier_all()
            P.dma("sync", lambda e: e.dma_start(out=t, in_=buf[0:rows, 0:cols]), qTAP)
            P.barrier_all()

        onesC = E(nc.sbuf_tensor("onesC", [128, 128], F32))
        selS = E(nc.sbuf_tensor("selS", [16, 2048], F32))
        w_sb = E(nc.sbuf_tensor("w_sb", [16, 2048], F32))
        fcbs = E(nc.sbuf_tensor("fcbs", [128, 256], F32))
        pjbs = E(nc.sbuf_tensor("pjbs", [128, 64], F32))
        pb = [E(nc.psum_tensor(f"pb{i}", [128, 512], F32)) for i in range(8)]

        P.dma("sync", lambda e: e.dma_start(out=selS[:], in_=selm_t.ap()), qC)
        for e_ in range(NEXP):
            P.dma("sync", lambda e, e_=e_: e.dma_start(out=fcbs[:, e_ * 32:(e_ + 1) * 32], in_=fcbi[e_]), qC)
            P.dma("sync", lambda e, e_=e_: e.dma_start(out=pjbs[:, e_ * 8:(e_ + 1) * 8], in_=pjbi[e_]), qC)
        P.v(lambda e: e.memset(w_sb[:], 0.0))
        P.v(lambda e: e.memset(onesC[:], 1.0))

        def ln_stats(src, xbuf, sqbuf, arena):
            slot_use = [None] * 4
            sq_use = [None, None]
            for dc in range(NDC):
                for tb in range(NTB):
                    i = dc * NTB + tb
                    sl = i % 4
                    w = [slot_use[sl]] if slot_use[sl] else []
                    ld = P.dma("sync", lambda e, dc=dc, tb=tb, sl=sl: e.dma_start(
                        out=xbuf[:, sl * 512:(sl + 1) * 512],
                        in_=src[dc, tb]), qX[sl], w)
                    sq = P.s(lambda e, sl=sl: e.activation(
                        sqbuf[:, (sl % 2) * 512:(sl % 2 + 1) * 512],
                        xbuf[:, sl * 512:(sl + 1) * 512], AF.Square),
                        [ld] + ([sq_use[sl % 2]] if sq_use[sl % 2] else []))
                    P.mm(tb, lambda e, tb=tb, sl=sl, dc=dc: e.matmul(
                        pb[tb][0:33, :], onesC[:, 0:33], xbuf[:, sl * 512:(sl + 1) * 512],
                        start=(dc == 0), stop=(dc == NDC - 1)), [ld])
                    tok = P.mm(4 + tb, lambda e, tb=tb, sl=sl, dc=dc: e.matmul(
                        pb[4 + tb][0:33, :], onesC[:, 0:33],
                        sqbuf[:, (sl % 2) * 512:(sl % 2 + 1) * 512],
                        start=(dc == 0), stop=(dc == NDC - 1)), [sq])
                    slot_use[sl] = tok
                    sq_use[sl % 2] = tok
            stats_done = (P.mmS, P.mmS.n)
            for tb in range(NTB):
                cl = slice(tb * TBW, (tb + 1) * TBW)
                P.rd("vector", tb, lambda e, tb=tb, cl=cl: e.tensor_scalar(
                    arena[0:1, cl], pb[tb][0:1, :], 1.0 / D, None, ALU.mult), [stats_done])
                P.rd("vector", tb, lambda e, tb=tb, cl=cl: e.tensor_scalar(
                    sqbuf[32:33, 0:512], pb[tb][32:33, :], 1.0 / D, None, ALU.mult))
                P.v(lambda e: e.tensor_tensor(sqbuf[32:33, 0:512], sqbuf[32:33, 0:512],
                                              sqbuf[32:33, 0:512], ALU.mult))
                P.rd("vector", 4 + tb, lambda e, tb=tb: e.tensor_scalar(
                    sqbuf[32:33, 512:1024], pb[4 + tb][32:33, :], 1.0 / D, EPS_LN,
                    ALU.mult, ALU.add))
                P.v(lambda e: e.tensor_tensor(sqbuf[32:33, 512:1024], sqbuf[32:33, 512:1024],
                                              sqbuf[32:33, 0:512], ALU.subtract))
                P.s(lambda e: e.activation(sqbuf[32:33, 0:512], sqbuf[32:33, 512:1024],
                                           AF.Sqrt), [(P.veS, P.veS.n)])
                P.v(lambda e, cl=cl: e.reciprocal(arena[32:33, cl], sqbuf[32:33, 0:512]),
                    [(P.scS, P.scS.n)])

        with ExitStack() as s1:
            EA = s1.enter_context
            xn = EA(nc.sbuf_tensor("xn", [128, NDC * TPC], BF16))
            qf = EA(nc.sbuf_tensor("qf", [128, NDC * TPC], BF16))
            kf = EA(nc.sbuf_tensor("kf", [128, 16 * 1024], BF16))
            vt = EA(nc.sbuf_tensor("vt", [128, 16 * 1024], BF16))
            qk = EA(nc.sbuf_tensor("qk", [128, TPC], BF16))
            xst = EA(nc.sbuf_tensor("xst", [128, 4 * 512], F32))
            wts = EA(nc.sbuf_tensor("wts", [128, 8 * 1024], BF16))
            scr = EA(nc.sbuf_tensor("scr", [128, 3 * 512], F32))
            rbsb = EA(nc.sbuf_tensor("rbsb", [128, 512], F32))
            stg = EA(nc.sbuf_tensor("stg", [128, 2 * 512], F32))
            kvpack = EA(nc.sbuf_tensor("kvpack", [128, 520], F32))
            kv_bf = EA(nc.sbuf_tensor("kv_bf", [128, 520], BF16))
            blkP = EA(nc.sbuf_tensor("blkP", [128, 128], BF16))
            bqs = EA(nc.sbuf_tensor("bqs", [128, 32], F32))
            bvb = EA(nc.sbuf_tensor("bvb", [128, D], F32))
            ptsb = EA(nc.sbuf_tensor("ptsb", [128, 64], F32))
            ones_bf = EA(nc.sbuf_tensor("ones_bf", [128, 1], BF16))
            arena = EA(nc.sbuf_tensor("arena", [48, 2048], F32))
            P.v(lambda e: e.memset(arena[:], 0.0))

            P.dma("sync", lambda e: e.dma_start(out=bqs[:], in_=bqkvo), qC)
            P.dma("sync", lambda e: e.dma_start(out=ptsb[0:64, :], in_=ptT), qC)
            P.dma("sync", lambda e: e.dma_start(out=ptsb[64:128, :], in_=ptT), qC)
            P.dma("sync", lambda e: e.dma_start(out=bvb[0:1, :], in_=bvrow), qC)
            cload = (qC, qC.n)
            P.v(lambda e: e.memset(ones_bf[:], 1.0))
            mz = P.v(lambda e: e.memset(blkP[:], 0.0))
            P.s(lambda e: e.activation(blkP[0:64, 0:64], ptsb[0:64, :], AF.Copy, scale=PNORM),
                [cload, mz])
            blkP_rdy = P.s(lambda e: e.activation(blkP[64:128, 64:128], ptsb[64:128, :],
                                                  AF.Copy, scale=PNORM), [cload])
            vz = (P.veS, P.veS.n)
            P.mm(6, lambda e: e.matmul(pb[6][:], onesC[0:1, :], bvb[0:1, 0:512],
                                       start=True, stop=True), [cload, vz])
            P.mm(7, lambda e: e.matmul(pb[7][:], onesC[0:1, :], bvb[0:1, 512:1024],
                                       start=True, stop=True), [cload, vz])
            P.rd("vector", 6, lambda e: e.tensor_copy(bvb[:, 0:512], pb[6][:]))
            bvb_rdy = P.rd("vector", 7, lambda e: e.tensor_copy(bvb[:, 512:1024], pb[7][:]))

            def load_wt(src, waits):
                last = None
                for dc in range(NDC):
                    last = P.dma("gpsimd", lambda e, dc=dc, src=src: e.dma_start(
                        out=wts[:, dc * 1024:(dc + 1) * 1024],
                        in_=src[dc * 128:(dc + 1) * 128, :]), qWT,
                        waits if dc == 0 else [])
                return last

            wq_rdy = load_wt(wqT, [])

            P.barrier_all()
            ln_stats(xfm, xst, scr, arena)
            P.barrier_all()
            tap("t_ar", arena, 48, 2048, F32)

            slot_use = [None] * 4
            for tb in range(NTB):
                bm, br_ = 4 + (tb % 2) * 2, 5 + (tb % 2) * 2
                P.mm(bm, lambda e, tb=tb, bm=bm: e.matmul(
                    pb[bm][:], onesC[0:1, :], arena[0:1, tb * TBW:(tb + 1) * TBW],
                    start=True, stop=True))
                P.mm(br_, lambda e, tb=tb, br_=br_: e.matmul(
                    pb[br_][:], onesC[32:33, :], arena[32:33, tb * TBW:(tb + 1) * TBW],
                    start=True, stop=True))
                for dc in range(NDC):
                    i = tb * NDC + dc
                    sl = i % 4
                    w = [slot_use[sl]] if slot_use[sl] else []
                    ld = P.dma("sync", lambda e, dc=dc, tb=tb, sl=sl: e.dma_start(
                        out=xst[:, sl * 512:(sl + 1) * 512],
                        in_=xfm[dc, tb]), qX[sl], w)
                    P.rd("vector", bm, lambda e, sl=sl, bm=bm: e.tensor_tensor(
                        scr[:, (sl % 2) * 512:(sl % 2 + 1) * 512],
                        xst[:, sl * 512:(sl + 1) * 512], pb[bm][:], ALU.subtract), [ld])
                    tok = P.rd("vector", br_, lambda e, sl=sl, dc=dc, tb=tb, br_=br_: e.tensor_tensor(
                        xn[:, dc * TPC + tb * TBW: dc * TPC + (tb + 1) * TBW],
                        scr[:, (sl % 2) * 512:(sl % 2 + 1) * 512], pb[br_][:], ALU.mult))
                    slot_use[sl] = tok
            P.barrier_all()
            tap("t_xn", xn, 128, NDC * TPC, BF16)

            qk_guard = None

            def proj(which, w_rdy, bias_off):
                nonlocal qk_guard
                for oc in range(NDC):
                    banks = [0, 1, 2, 3] if oc % 2 == 0 else [4, 5, 6, 7]
                    for tb in range(NTB):
                        bk = banks[tb]
                        def mmg2(e, oc=oc, tb=tb, bk=bk):
                            last = None
                            for dc in range(NDC):
                                last = e.matmul(
                                    pb[bk][:],
                                    wts[:, dc * 1024 + oc * 128: dc * 1024 + (oc + 1) * 128],
                                    xn[:, dc * TPC + tb * TBW: dc * TPC + (tb + 1) * TBW],
                                    start=(dc == 0), stop=(dc == NDC - 1))
                            return last
                        P.mm(bk, mmg2, [w_rdy])
                    for tb in range(NTB):
                        bk = banks[tb]
                        w = [qk_guard] if qk_guard else []
                        P.rd("scalar", bk, lambda e, oc=oc, tb=tb, bk=bk: e.add(
                            qk[:, tb * TBW:(tb + 1) * TBW], pb[bk][:],
                            bqs[:, bias_off + oc: bias_off + oc + 1]), w)
                    if which == "q":
                        for tb in range(NTB):
                            bk = banks[tb]
                            tok = P.mm(bk, lambda e, tb=tb, bk=bk: e.matmul(
                                pb[bk][:], blkP[:], qk[:, tb * TBW:(tb + 1) * TBW],
                                start=True, stop=True), [blkP_rdy])
                            P.rd("vector", bk, lambda e, oc=oc, tb=tb, bk=bk: e.tensor_scalar(
                                qf[:, oc * TPC + tb * TBW: oc * TPC + (tb + 1) * TBW],
                                pb[bk][:], 0.0, EPS_FEAT, ALU.max, ALU.add))
                        qk_guard = tok
                    else:
                        for tb4 in range(NTB):
                            bk = banks[tb4]
                            def mmk(e, oc=oc, tb4=tb4, bk=bk):
                                last = None
                                for j in range(4):
                                    tc = tb4 * 4 + j
                                    last = e.matmul(
                                        pb[bk][:, j * 128:(j + 1) * 128],
                                        qk[:, tc * 128:(tc + 1) * 128], blkP[:],
                                        start=True, stop=True)
                                return last
                            tok = P.mm(bk, mmk, [blkP_rdy])
                            def evk(e, oc=oc, tb4=tb4, bk=bk):
                                last = None
                                for j in range(4):
                                    tc = tb4 * 4 + j
                                    last = e.tensor_scalar(
                                        kf[:, tc * 1024 + oc * 128: tc * 1024 + (oc + 1) * 128],
                                        pb[bk][:, j * 128:(j + 1) * 128], 0.0, EPS_FEAT,
                                        ALU.max, ALU.add)
                                return last
                            P.rd("vector", bk, evk)
                        qk_guard = tok

            proj("q", wq_rdy, 0)
            wk_rdy = load_wt(wkT, [(P.mmS, P.mmS.n)])
            proj("k", wk_rdy, 8)
            wv_rdy = load_wt(wvT, [(P.mmS, P.mmS.n)])

            for tc in range(16):
                for half in range(2):
                    bk = 4 * (tc % 2) + half
                    def mmv(e, tc=tc, half=half, bk=bk):
                        last = None
                        for dc in range(NDC):
                            last = e.matmul(
                                pb[bk][:],
                                xn[:, dc * TPC + tc * 128: dc * TPC + (tc + 1) * 128],
                                wts[:, dc * 1024 + half * 512: dc * 1024 + (half + 1) * 512],
                                start=(dc == 0), stop=(dc == NDC - 1))
                        return last
                    P.mm(bk, mmv, [wv_rdy])
                    P.rd("vector", bk, lambda e, tc=tc, half=half, bk=bk: e.scalar_tensor_tensor(
                        vt[:, tc * 1024 + half * 512: tc * 1024 + (half + 1) * 512],
                        pb[bk][:], 1.0, bvb[:, half * 512:(half + 1) * 512],
                        ALU.mult, ALU.add), [bvb_rdy])
            wo_rdy = load_wt(woT, [(P.mmS, P.mmS.n)])
            P.barrier_all()
            tap("t_qf", qf, 128, NDC * TPC, BF16)
            tap("t_kf", kf, 128, 16 * 1024, BF16)
            tap("t_vt", vt, 128, 16 * 1024, BF16)

            for hp in range(NDC):
                bk = hp % 2
                for par in range(2):
                    base = par * 64
                    h_ = hp * 2 + par
                    def mmkv(e, h_=h_, base=base, bk=bk):
                        last = None
                        for tc in range(16):
                            last = e.matmul(
                                pb[bk][base:base + 64, 0:64],
                                kf[:, tc * 1024 + h_ * 64: tc * 1024 + (h_ + 1) * 64],
                                vt[:, tc * 1024 + h_ * 64: tc * 1024 + (h_ + 1) * 64],
                                start=(tc == 0), stop=(tc == 15))
                        return last
                    P.mm(bk, mmkv)
                def mmks(e, hp=hp):
                    last = None
                    for tc in range(16):
                        last = e.matmul(
                            pb[2][:, hp:hp + 1],
                            kf[:, tc * 1024 + hp * 128: tc * 1024 + (hp + 1) * 128],
                            ones_bf[:], start=(tc == 0), stop=(tc == 15))
                    return last
                P.mm(2, mmks)
                P.rd("vector", bk, lambda e, hp=hp, bk=bk: e.tensor_copy(
                    kvpack[:, hp * 65: hp * 65 + 64], pb[bk][:, 0:64]))
            P.rd("vector", 2, lambda e: e.tensor_copy(kvpack[:, 64:520:65], pb[2][:, 0:8]))
            kv_done = (P.veS, P.veS.n)
            P.dma("gpsimd", lambda e: e.dma_start(out=kvin, in_=kvpack[:]), qKV, [kv_done])
            P.g(lambda e: e.collective_compute(
                "AllReduce", ALU.add, replica_groups=[[0, 1], [2, 3], [4, 5], [6, 7]],
                ins=[kvin.opt()], outs=[kvout.opt()]), [(qKV, qKV.n)], [(ccS, 1)])
            kvld = P.dma("gpsimd", lambda e: e.dma_start(out=kvpack[:], in_=kvout), qKV, [(ccS, 1)])
            kvbf_rdy = P.v(lambda e: e.tensor_copy(kv_bf[:], kvpack[:]), [kvld])
            P.barrier_all()
            tap("t_kv", kvpack, 128, 520, F32)

            rden_guard = None
            rbsE_guard = None
            rbsO_guard = None
            for hp in range(NDC):
                for tb in range(NTB):
                    i = hp * NTB + tb
                    bkE = i % 2
                    bkO = 2 + i % 2
                    bkD = 4 + i % 2
                    c0 = hp * TPC + tb * TBW
                    P.mm(bkE, lambda e, hp=hp, c0=c0, bkE=bkE: e.matmul(
                        pb[bkE][0:65, :], kv_bf[0:64, hp * 65:(hp + 1) * 65],
                        qf[0:64, c0:c0 + TBW], start=True, stop=True))
                    P.mm(bkO, lambda e, hp=hp, c0=c0, bkO=bkO: e.matmul(
                        pb[bkO][64:128, :], kv_bf[64:128, hp * 65: hp * 65 + 64],
                        qf[64:128, c0:c0 + TBW], start=True, stop=True))
                    P.mm(bkD, lambda e, hp=hp, c0=c0, bkD=bkD: e.matmul(
                        pb[bkD][64:65, :], kv_bf[64:128, hp * 65 + 64: hp * 65 + 65],
                        qf[64:128, c0:c0 + TBW], start=True, stop=True))
                    wrd = [rden_guard] if rden_guard else []
                    P.rd("vector", bkE, lambda e, bkE=bkE: e.tensor_scalar(
                        scr[64:65, 0:512], pb[bkE][64:65, :], EPS_ATT, None, ALU.add), wrd)
                    P.v(lambda e: e.reciprocal(scr[64:65, 0:512], scr[64:65, 0:512]))
                    P.rd("vector", bkD, lambda e, bkD=bkD: e.tensor_scalar(
                        scr[64:65, 512:1024], pb[bkD][64:65, :], EPS_ATT, None, ALU.add))
                    rcp = P.v(lambda e: e.reciprocal(scr[64:65, 512:1024], scr[64:65, 512:1024]))
                    bE = P.mm(6, lambda e: e.matmul(
                        pb[6][0:64, :], onesC[64:65, 0:64], scr[64:65, 0:512],
                        start=True, stop=True), [rcp])
                    bO = P.mm(7, lambda e: e.matmul(
                        pb[7][64:128, :], onesC[64:65, 0:64], scr[64:65, 512:1024],
                        start=True, stop=True), [rcp])
                    rden_guard = bO
                    rbsE = P.rd("scalar", 6, lambda e: e.copy(rbsb[0:64, :], pb[6][0:64, :]),
                                [rbsE_guard] if rbsE_guard else [])
                    rbsO = P.rd("scalar", 7, lambda e: e.copy(rbsb[64:128, :], pb[7][64:128, :]),
                                [rbsO_guard] if rbsO_guard else [])
                    rbsE_guard = P.rd("vector", bkE, lambda e, c0=c0, bkE=bkE: e.tensor_tensor(
                        qf[0:64, c0:c0 + TBW], pb[bkE][0:64, :], rbsb[0:64, :],
                        ALU.mult), [rbsE])
                    rbsO_guard = P.rd("vector", bkO, lambda e, c0=c0, bkO=bkO: e.tensor_tensor(
                        qf[64:128, c0:c0 + TBW], pb[bkO][64:128, :], rbsb[64:128, :],
                        ALU.mult), [rbsO])
            P.barrier_all()
            tap("t_att", qf, 128, NDC * TPC, BF16)

            slot_use = [None] * 4
            store_use = [None, None]
            for oc in range(NDC):
                banks = [0, 1, 2, 3] if oc % 2 == 0 else [4, 5, 6, 7]
                for tb in range(NTB):
                    bk = banks[tb]
                    def mmo(e, oc=oc, tb=tb, bk=bk):
                        last = None
                        for dc in range(NDC):
                            last = e.matmul(
                                pb[bk][:],
                                wts[:, dc * 1024 + oc * 128: dc * 1024 + (oc + 1) * 128],
                                qf[:, dc * TPC + tb * TBW: dc * TPC + (tb + 1) * TBW],
                                start=(dc == 0), stop=(dc == NDC - 1))
                        return last
                    P.mm(bk, mmo, [wo_rdy])
                for tb in range(NTB):
                    bk = banks[tb]
                    i = oc * NTB + tb
                    sl = i % 4
                    ssl = i % 2
                    w = [slot_use[sl]] if slot_use[sl] else []
                    ld = P.dma("sync", lambda e, oc=oc, tb=tb, sl=sl: e.dma_start(
                        out=xst[:, sl * 512:(sl + 1) * 512],
                        in_=xfm[oc, tb]), qX[sl], w)
                    wst = [ld] + ([store_use[ssl]] if store_use[ssl] else [])
                    ev = P.rd("vector", bk, lambda e, oc=oc, sl=sl, ssl=ssl, bk=bk: e.scalar_tensor_tensor(
                        stg[:, ssl * 512:(ssl + 1) * 512], pb[bk][:],
                        bqs[:, 24 + oc: 25 + oc], xst[:, sl * 512:(sl + 1) * 512],
                        ALU.add, ALU.add), wst)
                    slot_use[sl] = ev
                    P.dma("sync", lambda e, oc=oc, tb=tb, ssl=ssl: e.dma_start(
                        out=x1d[oc, tb],
                        in_=stg[:, ssl * 512:(ssl + 1) * 512]), qST[ssl], [ev])
                    store_use[ssl] = P.dma("sync", lambda e, oc=oc, tb=tb, ssl=ssl: e.dma_start(
                        out=ysfm[oc, tb],
                        in_=stg[:, ssl * 512:(ssl + 1) * 512]), qST[ssl], [])
            P.barrier_all()
            P.flush()

        s23 = ExitStack()
        w1s = s23.enter_context(nc.sbuf_tensor("w1s", [128, NDC * HID], BF16))
        qPC = [P.ctr(f"qPC{i}") for i in range(NEXP)]
        qFW2 = P.ctr("qFW2")
        with s23, ExitStack() as s2:
            EG = s2.enter_context
            xst2 = EG(nc.sbuf_tensor("xst2", [128, 4 * 512], F32))
            scr2 = EG(nc.sbuf_tensor("scr2", [128, 2 * 512], F32))
            xn2fs = EG(nc.sbuf_tensor("xn2fs", [128, 2 * 512], F32))
            xn2bst = EG(nc.sbuf_tensor("xn2bst", [128, 2 * 512], BF16))
            gsb = EG(nc.sbuf_tensor("gsb", [128, 64], F32))
            lg_sb = EG(nc.sbuf_tensor("lg_sb", [16, 2048], F32))
            mp1 = EG(nc.sbuf_tensor("mp1", [8, 2048], F32))
            mp2 = EG(nc.sbuf_tensor("mp2", [8, 2048], F32))
            lc1 = EG(nc.sbuf_tensor("lc1", [8, 2048], F32))
            lc2 = EG(nc.sbuf_tensor("lc2", [8, 2048], F32))
            tpl = EG(nc.sbuf_tensor("tpl", [8, 2048], F32))
            zz = EG(nc.sbuf_tensor("zz", [8, 2048], F32))
            lgt = EG(nc.sbuf_tensor("lgt", [128, 16], F32))
            rt = EG(nc.sbuf_tensor("rt", [128, 96], F32))
            cnt = EG(nc.sbuf_tensor("cnt", [8, 32], F32))
            cmsb = EG(nc.sbuf_tensor("cmsb", [8, 16], F32))
            arena = EG(nc.sbuf_tensor("arena2", [48, 2048], F32))
            identf = EG(nc.sbuf_tensor("identf", [128, 128], F32))

            P.v(lambda e: e.memset(arena[:], 0.0))
            P.dma("sync", lambda e: e.dma_start(out=identf[:], in_=identf_t.ap()), qC)
            P.dma("sync", lambda e: e.dma_start(
                out=gsb[:, 0:64].rearrange("p (a e) -> p a e", e=8),
                in_=gw.rearrange("(a p) e -> p a e", p=128)), qC)
            P.dma("sync", lambda e: e.dma_start(out=cmsb[:, 0:8], in_=cmself), qC)
            P.dma("sync", lambda e: e.dma_start(out=cmsb[:, 8:16], in_=cmprev), qC)

            ln_stats(x1d, xst2, scr2, arena)
            P.barrier_all()

            slot_use = [None] * 4
            nstore_use = [None, None]
            lg_guard = [None, None]
            for tb in range(NTB):
                bm, br_, bl = tb % 2, 2 + tb % 2, 4 + tb % 2
                P.mm(bm, lambda e, tb=tb, bm=bm: e.matmul(
                    pb[bm][:], onesC[0:1, :], arena[0:1, tb * TBW:(tb + 1) * TBW],
                    start=True, stop=True))
                P.mm(br_, lambda e, tb=tb, br_=br_: e.matmul(
                    pb[br_][:], onesC[32:33, :], arena[32:33, tb * TBW:(tb + 1) * TBW],
                    start=True, stop=True))
                for dc in range(NDC):
                    i = tb * NDC + dc
                    sl = i % 4
                    xsl = dc % 2
                    w = [slot_use[sl]] if slot_use[sl] else []
                    ld = P.dma("sync", lambda e, dc=dc, tb=tb, sl=sl: e.dma_start(
                        out=xst2[:, sl * 512:(sl + 1) * 512],
                        in_=x1d[dc, tb]), qX[sl], w)
                    P.rd("vector", bm, lambda e, sl=sl, bm=bm, xsl=xsl: e.tensor_tensor(
                        scr2[:, xsl * 512:(xsl + 1) * 512],
                        xst2[:, sl * 512:(sl + 1) * 512], pb[bm][:], ALU.subtract), [ld])
                    wg = [lg_guard[xsl]] if lg_guard[xsl] else []
                    mtok = P.rd("vector", br_, lambda e, xsl=xsl, br_=br_: e.tensor_tensor(
                        xn2fs[:, xsl * 512:(xsl + 1) * 512],
                        scr2[:, xsl * 512:(xsl + 1) * 512], pb[br_][:], ALU.mult), wg)
                    slot_use[sl] = mtok
                    ct = P.s(lambda e, xsl=xsl: e.copy(
                        xn2bst[:, xsl * 512:(xsl + 1) * 512],
                        xn2fs[:, xsl * 512:(xsl + 1) * 512]),
                        [mtok] + ([nstore_use[xsl]] if nstore_use[xsl] else []))
                    nstore_use[xsl] = P.dma("sync", lambda e, dc=dc, tb=tb, xsl=xsl: e.dma_start(
                        out=xn2d[dc, tb],
                        in_=xn2bst[:, xsl * 512:(xsl + 1) * 512]), qN[xsl], [ct])
                    lg_guard[xsl] = P.mm(bl, lambda e, dc=dc, xsl=xsl, bl=bl: e.matmul(
                        pb[bl][0:8, :], gsb[:, dc * 8:(dc + 1) * 8],
                        xn2fs[:, xsl * 512:(xsl + 1) * 512],
                        start=(dc == 0), stop=(dc == NDC - 1)), [mtok, (qC, qC.n)])
                P.rd("vector", bl, lambda e, tb=tb, bl=bl: e.tensor_copy(
                    lg_sb[0:8, tb * TBW:(tb + 1) * TBW], pb[bl][0:8, :]))
            P.barrier_all()

            P.v(lambda e: e.memset(zz[:], 0.0))
            P.v(lambda e: e.memset(rt[:], 0.0))
            for tt in range(16):
                P.mm(6, lambda e, tt=tt: e.transpose(
                    pb[6][:, 0:8], lg_sb[0:8, tt * 128:(tt + 1) * 128], identf[0:8, 0:8]))
                P.rd("vector", 6, lambda e: e.tensor_copy(lgt[:, 0:8], pb[6][:, 0:8]))
                P.v(lambda e: e.tensor_reduce(rt[:, 64:65], lgt[:, 0:8], AX.X, ALU.max))
                P.v(lambda e: e.tensor_scalar(rt[:, 0:8], lgt[:, 0:8], rt[:, 64:65], None, ALU.is_ge))
                P.v(lambda e: e.tensor_tensor_scan(rt[:, 8:16], rt[:, 0:8], rt[:, 80:88],
                                                   0.0, ALU.add, ALU.add))
                P.v(lambda e: e.tensor_scalar(rt[:, 16:24], rt[:, 8:16], 1.0, None, ALU.is_le))
                P.v(lambda e: e.tensor_tensor(rt[:, 24:32], rt[:, 0:8], rt[:, 16:24], ALU.mult))
                P.v(lambda e: e.scalar_tensor_tensor(rt[:, 32:40], rt[:, 24:32], -1e30,
                                                     lgt[:, 0:8], ALU.mult, ALU.add))
                P.v(lambda e: e.tensor_reduce(rt[:, 65:66], rt[:, 32:40], AX.X, ALU.max))
                P.v(lambda e: e.tensor_tensor(rt[:, 66:67], rt[:, 64:65], rt[:, 65:66], ALU.subtract))
                P.s(lambda e: e.activation(rt[:, 67:68], rt[:, 66:67], AF.Sigmoid), [(P.veS, P.veS.n)])
                sg = P.s(lambda e: e.activation(rt[:, 68:69], rt[:, 66:67], AF.Sigmoid, scale=-1.0))
                P.v(lambda e: e.tensor_scalar(rt[:, 40:48], rt[:, 32:40], rt[:, 65:66], None, ALU.is_ge))
                P.v(lambda e: e.tensor_tensor_scan(rt[:, 48:56], rt[:, 40:48], rt[:, 80:88],
                                                   0.0, ALU.add, ALU.add))
                P.v(lambda e: e.tensor_scalar(rt[:, 16:24], rt[:, 48:56], 1.0, None, ALU.is_le))
                P.v(lambda e: e.tensor_tensor(rt[:, 56:64], rt[:, 40:48], rt[:, 16:24], ALU.mult))
                P.v(lambda e: e.tensor_scalar(rt[:, 0:8], rt[:, 24:32], rt[:, 67:68], None, ALU.mult), [sg])
                P.v(lambda e: e.tensor_scalar(rt[:, 8:16], rt[:, 56:64], rt[:, 68:69], None, ALU.mult))
                P.mm(6, lambda e: e.transpose(pb[6][0:8, 0:128], rt[:, 0:8], identf[:]),
                     [(P.veS, P.veS.n)])
                P.rd("vector", 6, lambda e, tt=tt: e.tensor_copy(
                    mp1[0:8, tt * 128:(tt + 1) * 128], pb[6][0:8, 0:128]))
                P.mm(7, lambda e: e.transpose(pb[7][0:8, 0:128], rt[:, 8:16], identf[:]),
                     [(P.veS, P.veS.n)])
                P.rd("vector", 7, lambda e, tt=tt: e.tensor_copy(
                    mp2[0:8, tt * 128:(tt + 1) * 128], pb[7][0:8, 0:128]))

            P.v(lambda e: e.tensor_scalar(tpl[:], mp1[:], 0.0, None, ALU.is_gt))
            P.v(lambda e: e.tensor_tensor_scan(lc1[:], tpl[:], zz[:], 0.0, ALU.add, ALU.add))
            P.v(lambda e: e.tensor_scalar(tpl[:], mp2[:], 0.0, None, ALU.is_gt))
            P.v(lambda e: e.tensor_tensor_scan(lc2[:], tpl[:], zz[:], 0.0, ALU.add, ALU.add))
            P.v(lambda e: e.tensor_scalar(cnt[:, 0:8], cmsb[:, 0:8], lc1[:, 2047:2048], None, ALU.mult))
            ctok = P.v(lambda e: e.tensor_scalar(cnt[:, 8:16], cmsb[:, 0:8], lc2[:, 2047:2048],
                                                 None, ALU.mult))
            P.dma("gpsimd", lambda e: e.dma_start(out=ccin[0:8, :], in_=cnt[:, 0:8]), qCC, [ctok])
            P.dma("gpsimd", lambda e: e.dma_start(out=ccin[8:16, :], in_=cnt[:, 8:16]), qCC)
            P.g(lambda e: e.collective_compute(
                "AllReduce", ALU.add, replica_groups=[list(range(NCORE))],
                ins=[ccin.opt()], outs=[ccout.opt()]), [(qCC, qCC.n), (ccS, 1)], [(ccS, 1)])
            l1 = P.dma("gpsimd", lambda e: e.dma_start(out=cnt[:, 16:24], in_=ccout[0:8, :]), qCC, [(ccS, 2)])
            l2 = P.dma("gpsimd", lambda e: e.dma_start(out=cnt[:, 24:32], in_=ccout[8:16, :]), qCC)
            P.v(lambda e: e.tensor_tensor(rt[0:8, 0:8], cnt[:, 16:24], cmsb[:, 8:16], ALU.mult), [l1, l2])
            P.v(lambda e: e.tensor_reduce(rt[0:8, 64:65], rt[0:8, 0:8], AX.X, ALU.add))
            P.v(lambda e: e.tensor_reduce(rt[0:8, 65:66], cnt[:, 16:24], AX.X, ALU.add))
            P.v(lambda e: e.tensor_tensor(rt[0:8, 8:16], cnt[:, 24:32], cmsb[:, 8:16], ALU.mult))
            P.v(lambda e: e.tensor_reduce(rt[0:8, 66:67], rt[0:8, 8:16], AX.X, ALU.add))
            P.v(lambda e: e.tensor_tensor(rt[0:8, 66:67], rt[0:8, 66:67], rt[0:8, 65:66], ALU.add))
            P.v(lambda e: e.tensor_scalar(tpl[:], lc1[:], rt[0:8, 64:65], None, ALU.add))
            P.v(lambda e: e.tensor_scalar(lc1[:], tpl[:], CAP, None, ALU.is_le))
            P.v(lambda e: e.tensor_tensor(mp1[:], lc1[:], mp1[:], ALU.mult))
            P.v(lambda e: e.tensor_scalar(tpl[:], lc2[:], rt[0:8, 66:67], None, ALU.add))
            P.v(lambda e: e.tensor_scalar(lc2[:], tpl[:], CAP, None, ALU.is_le))
            P.v(lambda e: e.tensor_tensor(mp2[:], lc2[:], mp2[:], ALU.mult))
            P.v(lambda e: e.tensor_tensor(w_sb[0:8, :], mp1[:], mp2[:], ALU.add))
            pc_tok = []
            for ex in range(NEXP):
                P.dma("gpsimd", lambda e, ex=ex: e.dma_start(out=cfc_bf[ex], in_=cfc[ex]), qPC[ex])
                pc_tok.append(P.dma("gpsimd", lambda e, ex=ex: e.dma_start(
                    out=cpj_bf[ex], in_=cpj[ex]), qPC[ex]))
            w1_rdy0 = None
            for dc in range(NDC):
                w1_rdy0 = P.dma("sync", lambda e, dc=dc: e.dma_start(
                    out=w1s[:, dc * HID:(dc + 1) * HID],
                    in_=cfc_bf[0, dc * 128:(dc + 1) * 128, :]), qFW,
                    [pc_tok[0]] if dc == 0 else [])
            P.barrier_all(exclude=tuple(qPC) + (qFW, qFW2))
            P.flush()
            s2.close()

            s3 = ExitStack()
            EH = s3.enter_context
            w2s = EH(nc.sbuf_tensor("w2s", [128, NHC * 1024], BF16))
            hbf = EH(nc.sbuf_tensor("hbf", [128, NHC * 512], BF16))
            xts = EH(nc.sbuf_tensor("xts", [128, 2 * NDC * 512], BF16))
            wbs = EH(nc.sbuf_tensor("wbs", [128, 2 * 512], F32))
            yst = EH(nc.sbuf_tensor("yst", [128, 2 * 512], F32))

            xts_use = [None, None]
            yst_use = [None, None]
            wbs_use = [None, None]
            p1_done = None
            p2_done = None
            hbf_guard = None
            w1_next = w1_rdy0

            def load_w2(ex, waits):
                tok = None
                for hc in range(NHC):
                    tok = P.dma("sync", lambda e, ex=ex, hc=hc: e.dma_start(
                        out=w2s[:, hc * 1024:(hc + 1) * 1024],
                        in_=cpj_bf[ex, hc * 128:(hc + 1) * 128, :]), qFW2,
                        waits if hc == 0 else [])
                return tok

            def load_w1(ex, waits):
                tok = None
                for dc in range(NDC):
                    tok = P.dma("sync", lambda e, ex=ex, dc=dc: e.dma_start(
                        out=w1s[:, dc * HID:(dc + 1) * HID],
                        in_=cfc_bf[ex, dc * 128:(dc + 1) * 128, :]), qFW,
                        waits if dc == 0 else [])
                return tok

            for ex in range(NEXP):
                w1_rdy = w1_next
                w2_rdy = load_w2(ex, [pc_tok[ex]] + ([p2_done] if p2_done else []))
                for ts in range(NTB):
                    sl = (ex * NTB + ts) % 2
                    ldx = None
                    for dc in range(NDC):
                        ldx = P.dma("sync", lambda e, ts=ts, dc=dc, sl=sl: e.dma_start(
                            out=xts[:, sl * 4096 + dc * 512: sl * 4096 + (dc + 1) * 512],
                            in_=xn2d[dc, ts]), qFX[sl],
                            ([xts_use[sl]] if xts_use[sl] and dc == 0 else []))
                    wb = P.mm(4 + sl, lambda e, ex=ex, ts=ts, sl=sl: e.matmul(
                        pb[4 + sl][:], selS[:, ex * 128:(ex + 1) * 128],
                        w_sb[:, ts * TBW:(ts + 1) * TBW], start=True, stop=True),
                        [wbs_use[sl]] if wbs_use[sl] else [])
                    wbr = P.rd("scalar", 4 + sl, lambda e, sl=sl: e.copy(
                        wbs[:, sl * 512:(sl + 1) * 512], pb[4 + sl][:]))
                    for hc in range(NHC):
                        bk = hc % 2
                        def mm1(e, hc=hc, sl=sl, bk=bk):
                            last = None
                            for dc in range(NDC):
                                last = e.matmul(
                                    pb[bk][:],
                                    w1s[:, dc * HID + hc * 128: dc * HID + (hc + 1) * 128],
                                    xts[:, sl * 4096 + dc * 512: sl * 4096 + (dc + 1) * 512],
                                    start=(dc == 0), stop=(dc == NDC - 1))
                            return last
                        P.mm(bk, mm1, [w1_rdy, ldx])
                        P.rd("scalar", bk, lambda e, ex=ex, hc=hc, bk=bk: e.activation(
                            hbf[:, hc * 512:(hc + 1) * 512], pb[bk][:], AF.Gelu,
                            bias=fcbs[:, ex * 32 + hc: ex * 32 + hc + 1]),
                            [hbf_guard] if hbf_guard else [])
                    gelu_all = (P.scS, P.scS.n)
                    if ts == NTB - 1:
                        p1_done = (P.mmS, P.mmS.n)
                        if ex + 1 < NEXP:
                            w1_next = load_w1(ex + 1, [p1_done, pc_tok[ex + 1]])
                    for dc in range(NDC):
                        bk = 2 + dc % 2
                        def mm2(e, dc=dc, bk=bk):
                            last = None
                            for hc in range(NHC):
                                last = e.matmul(
                                    pb[bk][:],
                                    w2s[:, hc * 1024 + dc * 128: hc * 1024 + (dc + 1) * 128],
                                    hbf[:, hc * 512:(hc + 1) * 512],
                                    start=(hc == 0), stop=(hc == NHC - 1))
                            return last
                        P.mm(bk, mm2, [gelu_all, w2_rdy])
                        ysl = dc % 2
                        ev = P.rd("vector", bk, lambda e, ex=ex, dc=dc, ysl=ysl, sl=sl, bk=bk: e.scalar_tensor_tensor(
                            yst[:, ysl * 512:(ysl + 1) * 512], pb[bk][:],
                            pjbs[:, ex * 8 + dc: ex * 8 + dc + 1],
                            wbs[:, sl * 512:(sl + 1) * 512], ALU.add, ALU.mult),
                            [wbr] + ([yst_use[ysl]] if yst_use[ysl] else []))
                        yst_use[ysl] = P.dma("gpsimd", lambda e, ex=ex, dc=dc, ts=ts, ysl=ysl: e.dma_start(
                            out=ysfm[dc, ts],
                            in_=yst[:, ysl * 512:(ysl + 1) * 512],
                            accum_op=ALU.add), qYP[ysl], [ev])
                    hbf_guard = (P.mmS, P.mmS.n)
                    xts_use[sl] = (P.mmS, P.mmS.n)
                    wbs_use[sl] = (P.veS, P.veS.n)
                p2_done = (P.mmS, P.mmS.n)
            P.barrier_all()
            s3.close()
            P.flush()

        P.emit("gpsimd", None, waits=[(qYP[0], qYP[0].n), (qYP[1], qYP[1].n),
                                      (qST[0], qST[0].n), (qST[1], qST[1].n)])
        P.flush()

    return nc


def kernel(L0_attn_x, L0_attn_Wq, L0_attn_bq, L0_attn_Wk, L0_attn_bk,
           L0_attn_Wv, L0_attn_bv, L0_attn_Wo, L0_attn_bo, L0_attn_P,
           L0_ffn_gate_w, L0_ffn_c_fc, L0_ffn_fc_bias, L0_ffn_c_proj,
           L0_ffn_proj_bias):
    x = np.ascontiguousarray(np.asarray(L0_attn_x, dtype=np.float32))
    B, N, D_ = x.shape
    X2 = x.reshape(B * N, D_)

    c = np.ascontiguousarray
    shared = dict(
        wqT=c(np.asarray(L0_attn_Wq, np.float32).T),
        wkT=c(np.asarray(L0_attn_Wk, np.float32).T),
        wvT=c(np.asarray(L0_attn_Wv, np.float32).T),
        woT=c(np.asarray(L0_attn_Wo, np.float32).T),
        bqkvo=c(np.concatenate(
            [np.asarray(b, np.float32).reshape(8, 128).T
             for b in (L0_attn_bq, L0_attn_bk, L0_attn_bv, L0_attn_bo)], axis=1)),
        bvrow=c(np.asarray(L0_attn_bv, np.float32).reshape(1, D_)),
        ptT=c(np.asarray(L0_attn_P, np.float32).T),
        gw=c(np.asarray(L0_ffn_gate_w, np.float32)),
        cfc=c(np.asarray(L0_ffn_c_fc, np.float32)),
        fcbi=c(np.stack([np.asarray(L0_ffn_fc_bias, np.float32)[e].reshape(32, 128).T
                         for e in range(NEXP)])),
        cpj=c(np.asarray(L0_ffn_c_proj, np.float32)),
        pjbi=c(np.stack([np.asarray(L0_ffn_proj_bias, np.float32)[e].reshape(8, 128).T
                         for e in range(NEXP)])),
    )

    in_maps = []
    for core in range(NCORE):
        sl = X2[core * TPC:(core + 1) * TPC]
        xfm_ = c(sl.T.reshape(NDC, 128, NTB, TBW).transpose(0, 2, 1, 3))
        cms_ = np.zeros((8, 8), np.float32)
        cms_[:, core] = 1.0
        cmp_ = np.zeros((8, 8), np.float32)
        cmp_[:, :core] = 1.0
        m = dict(shared)
        m.update(xfm=xfm_, cmself=cms_, cmprev=cmp_)
        in_maps.append(m)

    nc = build_program()
    res = run_bass_kernel_spmd(nc, in_maps, core_ids=list(range(NCORE)))

    out = np.empty((B * N, D_), np.float32)
    for core in range(NCORE):
        yf = np.asarray(res.results[core]["ysfm"])
        out[core * TPC:(core + 1) * TPC] = yf.transpose(0, 2, 1, 3).reshape(D_, TPC).T
    return out.reshape(B, N, D_)


# version 12 (final)
# speedup vs baseline: 1.2193x; 1.2057x over previous
import numpy as np
import ml_dtypes
from contextlib import ExitStack
from types import SimpleNamespace

import concourse.bass as bass
import concourse.mybir as mybir
from concourse.bass_utils import run_bass_kernel_spmd

F32 = mybir.dt.float32
BF16 = mybir.dt.bfloat16
AF = mybir.ActivationFunctionType
ALU = mybir.AluOpType
AX = mybir.AxisListType

NCORE = 8
D = 1024
TPC = 2048
NTB = 4
TBW = 512
NDC = 8
NEXP = 8
HID = 4096
NHC = 32
CAP = 5120.0
EPS_LN = 1e-5
EPS_ATT = 1e-6
EPS_FEAT = 1e-6
PNORM = 1.0 / (64.0 ** 0.25)

ENGINES = ("sync", "gpsimd", "tensor", "vector", "scalar")

_TAPS = ()


class K:

    def __init__(self, nc, stk):
        self.nc = nc
        self.stk = stk
        self.seq = []
        self.ctrs = []
        self.veS = self.ctr("veS")
        self.scS = self.ctr("scS")
        self.mmS = self.ctr("mmS")
        self.bw = [None] * 8
        self.br = [[] for _ in range(8)]

    def ctr(self, name):
        h = self.stk.enter_context(self.nc.semaphore(name))
        c = SimpleNamespace(h=h, n=0, name=name)
        self.ctrs.append(c)
        return c

    def emit(self, eng, fn, waits=(), incs=()):
        seen = {}
        for c, v in waits:
            assert 0 <= v <= c.n, f"wait {c.name}>={v} but only {c.n} emitted"
            if v > 0:
                prev = seen.get(id(c))
                seen[id(c)] = (c.h, v if prev is None else max(v, prev[1]))
        ws = list(seen.values())
        rincs = []
        for c, k in incs:
            c.n += k
            rincs.append((c.h, k))
        self.seq.append((eng, ws, fn, rincs))

    def v(self, fn, waits=()):
        w = list(waits)
        if self.veS.n:
            w.append((self.veS, self.veS.n))
        self.emit("vector", fn, w, [(self.veS, 1)])
        return (self.veS, self.veS.n)

    def s(self, fn, waits=()):
        w = list(waits)
        if self.scS.n:
            w.append((self.scS, self.scS.n))
        self.emit("scalar", fn, w, [(self.scS, 1)])
        return (self.scS, self.scS.n)

    def dma(self, eng, fn, c, waits=()):
        self.emit(eng, fn, waits, [(c, 16)])
        return (c, c.n)

    def g(self, fn, waits=(), incs=()):
        self.emit("gpsimd", fn, waits, incs)

    def mm(self, bank, fn, waits=()):
        w = list(waits) + self.br[bank]
        if self.bw[bank] is not None:
            w.append(self.bw[bank])
        self.emit("tensor", fn, w, [(self.mmS, 1)])
        tok = (self.mmS, self.mmS.n)
        self.bw[bank] = tok
        self.br[bank] = []
        return tok

    def rd(self, eng, bank, fn, waits=()):
        w = list(waits)
        if self.bw[bank] is not None:
            w.append(self.bw[bank])
        tok = self.v(fn, w) if eng == "vector" else self.s(fn, w)
        self.br[bank].append(tok)
        return tok

    def snap(self):
        return [(c, c.n) for c in self.ctrs]

    def barrier_all(self, exclude=()):
        ex = set(id(c) for c in exclude)
        sn = [(c, v) for c, v in self.snap() if id(c) not in ex]
        for e in ENGINES:
            self.emit(e, None, waits=[(c, v) for c, v in sn if v > 0])
        self.bw = [None] * 8
        self.br = [[] for _ in range(8)]

    def flush(self):
        with self.nc.Block(no_gpsimd_drain=True) as block:
            for en in ENGINES:
                def run(engine, en=en):
                    for eng, ws, fn, incs in self.seq:
                        if eng != en:
                            continue
                        for h, v in ws:
                            engine.wait_ge(h, v)
                        inst = fn(engine) if fn is not None else None
                        for h, kk in incs:
                            inst.then_inc(h, kk)
                getattr(block, en)(run)
        self.seq = []


def build_program():
    nc = bass.Bass("TRN2", target_bir_lowering=False)

    def inp(name, shape, dt=F32):
        return nc.dram_tensor(name, list(shape), dt, kind="ExternalInput").ap()

    xfm = inp("xfm", (NDC, NTB, 128, TBW))
    wqT = inp("wqT", (D, D))
    wkT = inp("wkT", (D, D))
    wvT = inp("wvT", (D, D))
    woT = inp("woT", (D, D))
    bqkvo = inp("bqkvo", (128, 32))
    bvrow = inp("bvrow", (1, D))
    ptT = inp("ptT", (64, 64))
    gw = inp("gw", (D, NEXP))
    cfc = inp("cfc", (NEXP, D, HID))
    fcbi = inp("fcbi", (NEXP, 128, 32))
    cpj = inp("cpj", (NEXP, HID, D))
    pjbi = inp("pjbi", (NEXP, 128, 8))
    cmself = inp("cmself", (8, 8))
    cmprev = inp("cmprev", (8, 8))

    identf_t = nc.inline_tensor(np.eye(128, dtype=np.float32), "identc")
    selm = np.zeros((16, 2048), np.float32)
    for r_ in range(16):
        selm[r_, r_ * 128:(r_ + 1) * 128] = 1.0
    selm_t = nc.inline_tensor(selm, "selmc")

    def intern(name, shape, dt=F32):
        kind = "ExternalOutput" if name in _TAPS else "Internal"
        return nc.dram_tensor(name, list(shape), dt, kind=kind).ap()

    x1d = intern("x1d", (NDC, NTB, 128, TBW))
    xn2d = intern("xn2d", (NDC, NTB, 128, TBW), BF16)

    kvin = intern("kvin", (128, 520))
    kvout = intern("kvout", (128, 520))
    ccin = intern("ccin", (16, 8))
    ccout = intern("ccout", (16, 8))
    ysfm = nc.dram_tensor("ysfm", [NDC, NTB, 128, TBW], F32, kind="ExternalOutput").ap()


    stk = ExitStack()
    with stk:
        E = stk.enter_context
        P = K(nc, stk)

        qC = P.ctr("qC")
        qX = [P.ctr(f"qX{i}") for i in range(4)]
        qWT = P.ctr("qWT")
        qKV = P.ctr("qKV")
        qST = [P.ctr(f"qST{i}") for i in range(2)]
        qN = [P.ctr(f"qN{i}") for i in range(2)]
        qCC = P.ctr("qCC")
        qFW = P.ctr("qFW")
        qFX = [P.ctr(f"qFX{i}") for i in range(2)]
        qYP = [P.ctr(f"qYP{i}") for i in range(2)]
        qFS = [P.ctr(f"qFS{i}") for i in range(2)]
        qOUT = P.ctr("qOUT")
        ccS = P.ctr("ccS")
        qTAP = P.ctr("qTAP")

        def tap(name, buf, rows, cols, dt=F32):
            if name not in _TAPS:
                return
            t = nc.dram_tensor(name, [rows, cols], dt, kind="ExternalOutput").ap()
            P.barrier_all()
            P.dma("sync", lambda e: e.dma_start(out=t, in_=buf[0:rows, 0:cols]), qTAP)
            P.barrier_all()

        onesC = E(nc.sbuf_tensor("onesC", [128, 128], F32))
        selS = E(nc.sbuf_tensor("selS", [16, 2048], F32))
        w_sb = E(nc.sbuf_tensor("w_sb", [16, 2048], F32))
        fcbs = E(nc.sbuf_tensor("fcbs", [128, 256], F32))
        pjbs = E(nc.sbuf_tensor("pjbs", [128, 64], F32))
        pb = [E(nc.psum_tensor(f"pb{i}", [128, 512], F32)) for i in range(8)]

        P.dma("sync", lambda e: e.dma_start(out=selS[:], in_=selm_t.ap()), qC)
        for e_ in range(NEXP):
            P.dma("sync", lambda e, e_=e_: e.dma_start(out=fcbs[:, e_ * 32:(e_ + 1) * 32], in_=fcbi[e_]), qC)
            P.dma("sync", lambda e, e_=e_: e.dma_start(out=pjbs[:, e_ * 8:(e_ + 1) * 8], in_=pjbi[e_]), qC)
        P.v(lambda e: e.memset(w_sb[:], 0.0))
        P.v(lambda e: e.memset(onesC[:], 1.0))

        def ln_stats(src, xbuf, sqbuf, arena):
            slot_use = [None] * 4
            sq_use = [None, None]
            for dc in range(NDC):
                for tb in range(NTB):
                    i = dc * NTB + tb
                    sl = i % 4
                    w = [slot_use[sl]] if slot_use[sl] else []
                    ld = P.dma("sync", lambda e, dc=dc, tb=tb, sl=sl: e.dma_start(
                        out=xbuf[:, sl * 512:(sl + 1) * 512],
                        in_=src[dc, tb]), qX[sl], w)
                    sq = P.s(lambda e, sl=sl: e.activation(
                        sqbuf[:, (sl % 2) * 512:(sl % 2 + 1) * 512],
                        xbuf[:, sl * 512:(sl + 1) * 512], AF.Square),
                        [ld] + ([sq_use[sl % 2]] if sq_use[sl % 2] else []))
                    P.mm(tb, lambda e, tb=tb, sl=sl, dc=dc: e.matmul(
                        pb[tb][0:33, :], onesC[:, 0:33], xbuf[:, sl * 512:(sl + 1) * 512],
                        start=(dc == 0), stop=(dc == NDC - 1)), [ld])
                    tok = P.mm(4 + tb, lambda e, tb=tb, sl=sl, dc=dc: e.matmul(
                        pb[4 + tb][0:33, :], onesC[:, 0:33],
                        sqbuf[:, (sl % 2) * 512:(sl % 2 + 1) * 512],
                        start=(dc == 0), stop=(dc == NDC - 1)), [sq])
                    slot_use[sl] = tok
                    sq_use[sl % 2] = tok
            stats_done = (P.mmS, P.mmS.n)
            for tb in range(NTB):
                cl = slice(tb * TBW, (tb + 1) * TBW)
                P.rd("vector", tb, lambda e, tb=tb, cl=cl: e.tensor_scalar(
                    arena[0:1, cl], pb[tb][0:1, :], 1.0 / D, None, ALU.mult), [stats_done])
                P.rd("vector", tb, lambda e, tb=tb, cl=cl: e.tensor_scalar(
                    sqbuf[32:33, 0:512], pb[tb][32:33, :], 1.0 / D, None, ALU.mult))
                P.v(lambda e: e.tensor_tensor(sqbuf[32:33, 0:512], sqbuf[32:33, 0:512],
                                              sqbuf[32:33, 0:512], ALU.mult))
                P.rd("vector", 4 + tb, lambda e, tb=tb: e.tensor_scalar(
                    sqbuf[32:33, 512:1024], pb[4 + tb][32:33, :], 1.0 / D, EPS_LN,
                    ALU.mult, ALU.add))
                P.v(lambda e: e.tensor_tensor(sqbuf[32:33, 512:1024], sqbuf[32:33, 512:1024],
                                              sqbuf[32:33, 0:512], ALU.subtract))
                P.s(lambda e: e.activation(sqbuf[32:33, 0:512], sqbuf[32:33, 512:1024],
                                           AF.Sqrt), [(P.veS, P.veS.n)])
                P.v(lambda e, cl=cl: e.reciprocal(arena[32:33, cl], sqbuf[32:33, 0:512]),
                    [(P.scS, P.scS.n)])

        with ExitStack() as s1:
            EA = s1.enter_context
            xn = EA(nc.sbuf_tensor("xn", [128, NDC * TPC], BF16))
            qf = EA(nc.sbuf_tensor("qf", [128, NDC * TPC], BF16))
            kf = EA(nc.sbuf_tensor("kf", [128, 16 * 1024], BF16))
            vt = EA(nc.sbuf_tensor("vt", [128, 16 * 1024], BF16))
            qk = EA(nc.sbuf_tensor("qk", [128, TPC], BF16))
            xst = EA(nc.sbuf_tensor("xst", [128, 4 * 512], F32))
            wts = EA(nc.sbuf_tensor("wts", [128, 8 * 1024], BF16))
            scr = EA(nc.sbuf_tensor("scr", [128, 3 * 512], F32))
            rbsb = EA(nc.sbuf_tensor("rbsb", [128, 512], F32))
            stg = EA(nc.sbuf_tensor("stg", [128, 2 * 512], F32))
            kvpack = EA(nc.sbuf_tensor("kvpack", [128, 520], F32))
            kv_bf = EA(nc.sbuf_tensor("kv_bf", [128, 520], BF16))
            blkP = EA(nc.sbuf_tensor("blkP", [128, 128], BF16))
            bqs = EA(nc.sbuf_tensor("bqs", [128, 32], F32))
            bvb = EA(nc.sbuf_tensor("bvb", [128, D], F32))
            ptsb = EA(nc.sbuf_tensor("ptsb", [128, 64], F32))
            ones_bf = EA(nc.sbuf_tensor("ones_bf", [128, 1], BF16))
            arena = EA(nc.sbuf_tensor("arena", [48, 2048], F32))
            P.v(lambda e: e.memset(arena[:], 0.0))

            P.dma("sync", lambda e: e.dma_start(out=bqs[:], in_=bqkvo), qC)
            P.dma("sync", lambda e: e.dma_start(out=ptsb[0:64, :], in_=ptT), qC)
            P.dma("sync", lambda e: e.dma_start(out=ptsb[64:128, :], in_=ptT), qC)
            P.dma("sync", lambda e: e.dma_start(out=bvb[0:1, :], in_=bvrow), qC)
            cload = (qC, qC.n)
            P.v(lambda e: e.memset(ones_bf[:], 1.0))
            mz = P.v(lambda e: e.memset(blkP[:], 0.0))
            P.s(lambda e: e.activation(blkP[0:64, 0:64], ptsb[0:64, :], AF.Copy, scale=PNORM),
                [cload, mz])
            blkP_rdy = P.s(lambda e: e.activation(blkP[64:128, 64:128], ptsb[64:128, :],
                                                  AF.Copy, scale=PNORM), [cload])
            vz = (P.veS, P.veS.n)
            P.mm(6, lambda e: e.matmul(pb[6][:], onesC[0:1, :], bvb[0:1, 0:512],
                                       start=True, stop=True), [cload, vz])
            P.mm(7, lambda e: e.matmul(pb[7][:], onesC[0:1, :], bvb[0:1, 512:1024],
                                       start=True, stop=True), [cload, vz])
            P.rd("vector", 6, lambda e: e.tensor_copy(bvb[:, 0:512], pb[6][:]))
            bvb_rdy = P.rd("vector", 7, lambda e: e.tensor_copy(bvb[:, 512:1024], pb[7][:]))

            def load_wt(src, waits):
                last = None
                for dc in range(NDC):
                    last = P.dma("gpsimd", lambda e, dc=dc, src=src: e.dma_start(
                        out=wts[:, dc * 1024:(dc + 1) * 1024],
                        in_=src[dc * 128:(dc + 1) * 128, :]), qWT,
                        waits if dc == 0 else [])
                return last

            wq_rdy = load_wt(wqT, [])

            P.barrier_all()
            ln_stats(xfm, xst, scr, arena)
            P.barrier_all()
            tap("t_ar", arena, 48, 2048, F32)

            slot_use = [None] * 4
            for tb in range(NTB):
                bm, br_ = 4 + (tb % 2) * 2, 5 + (tb % 2) * 2
                P.mm(bm, lambda e, tb=tb, bm=bm: e.matmul(
                    pb[bm][:], onesC[0:1, :], arena[0:1, tb * TBW:(tb + 1) * TBW],
                    start=True, stop=True))
                P.mm(br_, lambda e, tb=tb, br_=br_: e.matmul(
                    pb[br_][:], onesC[32:33, :], arena[32:33, tb * TBW:(tb + 1) * TBW],
                    start=True, stop=True))
                for dc in range(NDC):
                    i = tb * NDC + dc
                    sl = i % 4
                    w = [slot_use[sl]] if slot_use[sl] else []
                    ld = P.dma("sync", lambda e, dc=dc, tb=tb, sl=sl: e.dma_start(
                        out=xst[:, sl * 512:(sl + 1) * 512],
                        in_=xfm[dc, tb]), qX[sl], w)
                    P.rd("vector", bm, lambda e, sl=sl, bm=bm: e.tensor_tensor(
                        scr[:, (sl % 2) * 512:(sl % 2 + 1) * 512],
                        xst[:, sl * 512:(sl + 1) * 512], pb[bm][:], ALU.subtract), [ld])
                    tok = P.rd("vector", br_, lambda e, sl=sl, dc=dc, tb=tb, br_=br_: e.tensor_tensor(
                        xn[:, dc * TPC + tb * TBW: dc * TPC + (tb + 1) * TBW],
                        scr[:, (sl % 2) * 512:(sl % 2 + 1) * 512], pb[br_][:], ALU.mult))
                    slot_use[sl] = tok
            P.barrier_all()
            tap("t_xn", xn, 128, NDC * TPC, BF16)

            qk_guard = None

            def proj(which, w_rdy, bias_off):
                nonlocal qk_guard
                for oc in range(NDC):
                    banks = [0, 1, 2, 3] if oc % 2 == 0 else [4, 5, 6, 7]
                    for tb in range(NTB):
                        bk = banks[tb]
                        def mmg2(e, oc=oc, tb=tb, bk=bk):
                            last = None
                            for dc in range(NDC):
                                last = e.matmul(
                                    pb[bk][:],
                                    wts[:, dc * 1024 + oc * 128: dc * 1024 + (oc + 1) * 128],
                                    xn[:, dc * TPC + tb * TBW: dc * TPC + (tb + 1) * TBW],
                                    start=(dc == 0), stop=(dc == NDC - 1))
                            return last
                        P.mm(bk, mmg2, [w_rdy])
                    for tb in range(NTB):
                        bk = banks[tb]
                        w = [qk_guard] if qk_guard else []
                        P.rd("scalar", bk, lambda e, oc=oc, tb=tb, bk=bk: e.add(
                            qk[:, tb * TBW:(tb + 1) * TBW], pb[bk][:],
                            bqs[:, bias_off + oc: bias_off + oc + 1]), w)
                    if which == "q":
                        for tb in range(NTB):
                            bk = banks[tb]
                            tok = P.mm(bk, lambda e, tb=tb, bk=bk: e.matmul(
                                pb[bk][:], blkP[:], qk[:, tb * TBW:(tb + 1) * TBW],
                                start=True, stop=True), [blkP_rdy])
                            P.rd("vector", bk, lambda e, oc=oc, tb=tb, bk=bk: e.tensor_scalar(
                                qf[:, oc * TPC + tb * TBW: oc * TPC + (tb + 1) * TBW],
                                pb[bk][:], 0.0, EPS_FEAT, ALU.max, ALU.add))
                        qk_guard = tok
                    else:
                        for tb4 in range(NTB):
                            bk = banks[tb4]
                            def mmk(e, oc=oc, tb4=tb4, bk=bk):
                                last = None
                                for j in range(4):
                                    tc = tb4 * 4 + j
                                    last = e.matmul(
                                        pb[bk][:, j * 128:(j + 1) * 128],
                                        qk[:, tc * 128:(tc + 1) * 128], blkP[:],
                                        start=True, stop=True)
                                return last
                            tok = P.mm(bk, mmk, [blkP_rdy])
                            def evk(e, oc=oc, tb4=tb4, bk=bk):
                                last = None
                                for j in range(4):
                                    tc = tb4 * 4 + j
                                    last = e.tensor_scalar(
                                        kf[:, tc * 1024 + oc * 128: tc * 1024 + (oc + 1) * 128],
                                        pb[bk][:, j * 128:(j + 1) * 128], 0.0, EPS_FEAT,
                                        ALU.max, ALU.add)
                                return last
                            P.rd("vector", bk, evk)
                        qk_guard = tok

            proj("q", wq_rdy, 0)
            wk_rdy = load_wt(wkT, [(P.mmS, P.mmS.n)])
            proj("k", wk_rdy, 8)
            wv_rdy = load_wt(wvT, [(P.mmS, P.mmS.n)])

            for tc in range(16):
                for half in range(2):
                    bk = 4 * (tc % 2) + half
                    def mmv(e, tc=tc, half=half, bk=bk):
                        last = None
                        for dc in range(NDC):
                            last = e.matmul(
                                pb[bk][:],
                                xn[:, dc * TPC + tc * 128: dc * TPC + (tc + 1) * 128],
                                wts[:, dc * 1024 + half * 512: dc * 1024 + (half + 1) * 512],
                                start=(dc == 0), stop=(dc == NDC - 1))
                        return last
                    P.mm(bk, mmv, [wv_rdy])
                    P.rd("vector", bk, lambda e, tc=tc, half=half, bk=bk: e.scalar_tensor_tensor(
                        vt[:, tc * 1024 + half * 512: tc * 1024 + (half + 1) * 512],
                        pb[bk][:], 1.0, bvb[:, half * 512:(half + 1) * 512],
                        ALU.mult, ALU.add), [bvb_rdy])
            wo_rdy = load_wt(woT, [(P.mmS, P.mmS.n)])
            P.barrier_all()
            tap("t_qf", qf, 128, NDC * TPC, BF16)
            tap("t_kf", kf, 128, 16 * 1024, BF16)
            tap("t_vt", vt, 128, 16 * 1024, BF16)

            for hp in range(NDC):
                bk = hp % 2
                for par in range(2):
                    base = par * 64
                    h_ = hp * 2 + par
                    def mmkv(e, h_=h_, base=base, bk=bk):
                        last = None
                        for tc in range(16):
                            last = e.matmul(
                                pb[bk][base:base + 64, 0:64],
                                kf[:, tc * 1024 + h_ * 64: tc * 1024 + (h_ + 1) * 64],
                                vt[:, tc * 1024 + h_ * 64: tc * 1024 + (h_ + 1) * 64],
                                start=(tc == 0), stop=(tc == 15))
                        return last
                    P.mm(bk, mmkv)
                def mmks(e, hp=hp):
                    last = None
                    for tc in range(16):
                        last = e.matmul(
                            pb[2][:, hp:hp + 1],
                            kf[:, tc * 1024 + hp * 128: tc * 1024 + (hp + 1) * 128],
                            ones_bf[:], start=(tc == 0), stop=(tc == 15))
                    return last
                P.mm(2, mmks)
                P.rd("vector", bk, lambda e, hp=hp, bk=bk: e.tensor_copy(
                    kvpack[:, hp * 65: hp * 65 + 64], pb[bk][:, 0:64]))
            P.rd("vector", 2, lambda e: e.tensor_copy(kvpack[:, 64:520:65], pb[2][:, 0:8]))
            kv_done = (P.veS, P.veS.n)
            P.dma("gpsimd", lambda e: e.dma_start(out=kvin, in_=kvpack[:]), qKV, [kv_done])
            P.g(lambda e: e.collective_compute(
                "AllReduce", ALU.add, replica_groups=[[0, 1], [2, 3], [4, 5], [6, 7]],
                ins=[kvin.opt()], outs=[kvout.opt()]), [(qKV, qKV.n)], [(ccS, 1)])
            kvld = P.dma("gpsimd", lambda e: e.dma_start(out=kvpack[:], in_=kvout), qKV, [(ccS, 1)])
            kvbf_rdy = P.v(lambda e: e.tensor_copy(kv_bf[:], kvpack[:]), [kvld])
            P.barrier_all()
            tap("t_kv", kvpack, 128, 520, F32)

            rden_guard = None
            rbsE_guard = None
            rbsO_guard = None
            for hp in range(NDC):
                for tb in range(NTB):
                    i = hp * NTB + tb
                    bkE = i % 2
                    bkO = 2 + i % 2
                    bkD = 4 + i % 2
                    c0 = hp * TPC + tb * TBW
                    P.mm(bkE, lambda e, hp=hp, c0=c0, bkE=bkE: e.matmul(
                        pb[bkE][0:65, :], kv_bf[0:64, hp * 65:(hp + 1) * 65],
                        qf[0:64, c0:c0 + TBW], start=True, stop=True))
                    P.mm(bkO, lambda e, hp=hp, c0=c0, bkO=bkO: e.matmul(
                        pb[bkO][64:128, :], kv_bf[64:128, hp * 65: hp * 65 + 64],
                        qf[64:128, c0:c0 + TBW], start=True, stop=True))
                    P.mm(bkD, lambda e, hp=hp, c0=c0, bkD=bkD: e.matmul(
                        pb[bkD][64:65, :], kv_bf[64:128, hp * 65 + 64: hp * 65 + 65],
                        qf[64:128, c0:c0 + TBW], start=True, stop=True))
                    wrd = [rden_guard] if rden_guard else []
                    P.rd("vector", bkE, lambda e, bkE=bkE: e.tensor_scalar(
                        scr[64:65, 0:512], pb[bkE][64:65, :], EPS_ATT, None, ALU.add), wrd)
                    P.v(lambda e: e.reciprocal(scr[64:65, 0:512], scr[64:65, 0:512]))
                    P.rd("vector", bkD, lambda e, bkD=bkD: e.tensor_scalar(
                        scr[64:65, 512:1024], pb[bkD][64:65, :], EPS_ATT, None, ALU.add))
                    rcp = P.v(lambda e: e.reciprocal(scr[64:65, 512:1024], scr[64:65, 512:1024]))
                    bE = P.mm(6, lambda e: e.matmul(
                        pb[6][0:64, :], onesC[64:65, 0:64], scr[64:65, 0:512],
                        start=True, stop=True), [rcp])
                    bO = P.mm(7, lambda e: e.matmul(
                        pb[7][64:128, :], onesC[64:65, 0:64], scr[64:65, 512:1024],
                        start=True, stop=True), [rcp])
                    rden_guard = bO
                    rbsE = P.rd("scalar", 6, lambda e: e.copy(rbsb[0:64, :], pb[6][0:64, :]),
                                [rbsE_guard] if rbsE_guard else [])
                    rbsO = P.rd("scalar", 7, lambda e: e.copy(rbsb[64:128, :], pb[7][64:128, :]),
                                [rbsO_guard] if rbsO_guard else [])
                    rbsE_guard = P.rd("vector", bkE, lambda e, c0=c0, bkE=bkE: e.tensor_tensor(
                        qf[0:64, c0:c0 + TBW], pb[bkE][0:64, :], rbsb[0:64, :],
                        ALU.mult), [rbsE])
                    rbsO_guard = P.rd("vector", bkO, lambda e, c0=c0, bkO=bkO: e.tensor_tensor(
                        qf[64:128, c0:c0 + TBW], pb[bkO][64:128, :], rbsb[64:128, :],
                        ALU.mult), [rbsO])
            P.barrier_all()
            tap("t_att", qf, 128, NDC * TPC, BF16)

            slot_use = [None] * 4
            store_use = [None, None]
            for oc in range(NDC):
                banks = [0, 1, 2, 3] if oc % 2 == 0 else [4, 5, 6, 7]
                for tb in range(NTB):
                    bk = banks[tb]
                    def mmo(e, oc=oc, tb=tb, bk=bk):
                        last = None
                        for dc in range(NDC):
                            last = e.matmul(
                                pb[bk][:],
                                wts[:, dc * 1024 + oc * 128: dc * 1024 + (oc + 1) * 128],
                                qf[:, dc * TPC + tb * TBW: dc * TPC + (tb + 1) * TBW],
                                start=(dc == 0), stop=(dc == NDC - 1))
                        return last
                    P.mm(bk, mmo, [wo_rdy])
                for tb in range(NTB):
                    bk = banks[tb]
                    i = oc * NTB + tb
                    sl = i % 4
                    ssl = i % 2
                    w = [slot_use[sl]] if slot_use[sl] else []
                    ld = P.dma("sync", lambda e, oc=oc, tb=tb, sl=sl: e.dma_start(
                        out=xst[:, sl * 512:(sl + 1) * 512],
                        in_=xfm[oc, tb]), qX[sl], w)
                    wst = [ld] + ([store_use[ssl]] if store_use[ssl] else [])
                    ev = P.rd("vector", bk, lambda e, oc=oc, sl=sl, ssl=ssl, bk=bk: e.scalar_tensor_tensor(
                        stg[:, ssl * 512:(ssl + 1) * 512], pb[bk][:],
                        bqs[:, 24 + oc: 25 + oc], xst[:, sl * 512:(sl + 1) * 512],
                        ALU.add, ALU.add), wst)
                    slot_use[sl] = ev
                    P.dma("sync", lambda e, oc=oc, tb=tb, ssl=ssl: e.dma_start(
                        out=x1d[oc, tb],
                        in_=stg[:, ssl * 512:(ssl + 1) * 512]), qST[ssl], [ev])
                    store_use[ssl] = P.dma("sync", lambda e, oc=oc, tb=tb, ssl=ssl: e.dma_start(
                        out=ysfm[oc, tb],
                        in_=stg[:, ssl * 512:(ssl + 1) * 512]), qST[ssl], [])
            P.barrier_all()
            P.flush()

        s23 = ExitStack()
        w1s = s23.enter_context(nc.sbuf_tensor("w1s", [128, NDC * HID], BF16))
        qFW2 = P.ctr("qFW2")
        with s23, ExitStack() as s2:
            EG = s2.enter_context
            xst2 = EG(nc.sbuf_tensor("xst2", [128, 4 * 512], F32))
            scr2 = EG(nc.sbuf_tensor("scr2", [128, 2 * 512], F32))
            xn2fs = EG(nc.sbuf_tensor("xn2fs", [128, 2 * 512], F32))
            xn2bst = EG(nc.sbuf_tensor("xn2bst", [128, 2 * 512], BF16))
            gsb = EG(nc.sbuf_tensor("gsb", [128, 64], F32))
            lg_sb = EG(nc.sbuf_tensor("lg_sb", [16, 2048], F32))
            mp1 = EG(nc.sbuf_tensor("mp1", [8, 2048], F32))
            mp2 = EG(nc.sbuf_tensor("mp2", [8, 2048], F32))
            lc1 = EG(nc.sbuf_tensor("lc1", [8, 2048], F32))
            lc2 = EG(nc.sbuf_tensor("lc2", [8, 2048], F32))
            tpl = EG(nc.sbuf_tensor("tpl", [8, 2048], F32))
            zz = EG(nc.sbuf_tensor("zz", [8, 2048], F32))
            lgt = EG(nc.sbuf_tensor("lgt", [128, 16], F32))
            rt = EG(nc.sbuf_tensor("rt", [128, 96], F32))
            cnt = EG(nc.sbuf_tensor("cnt", [8, 32], F32))
            cmsb = EG(nc.sbuf_tensor("cmsb", [8, 16], F32))
            arena = EG(nc.sbuf_tensor("arena2", [48, 2048], F32))
            identf = EG(nc.sbuf_tensor("identf", [128, 128], F32))

            P.v(lambda e: e.memset(arena[:], 0.0))
            w1_rdy0 = None
            for dc in range(NDC):
                w1_rdy0 = P.dma("gpsimd", lambda e, dc=dc: e.dma_start(
                    out=w1s[:, dc * HID:(dc + 1) * HID],
                    in_=cfc[0, dc * 128:(dc + 1) * 128, :]), qFW)
            P.dma("sync", lambda e: e.dma_start(out=identf[:], in_=identf_t.ap()), qC)
            P.dma("sync", lambda e: e.dma_start(
                out=gsb[:, 0:64].rearrange("p (a e) -> p a e", e=8),
                in_=gw.rearrange("(a p) e -> p a e", p=128)), qC)
            P.dma("sync", lambda e: e.dma_start(out=cmsb[:, 0:8], in_=cmself), qC)
            P.dma("sync", lambda e: e.dma_start(out=cmsb[:, 8:16], in_=cmprev), qC)

            ln_stats(x1d, xst2, scr2, arena)
            P.barrier_all()

            slot_use = [None] * 4
            nstore_use = [None, None]
            lg_guard = [None, None]
            for tb in range(NTB):
                bm, br_, bl = tb % 2, 2 + tb % 2, 4 + tb % 2
                P.mm(bm, lambda e, tb=tb, bm=bm: e.matmul(
                    pb[bm][:], onesC[0:1, :], arena[0:1, tb * TBW:(tb + 1) * TBW],
                    start=True, stop=True))
                P.mm(br_, lambda e, tb=tb, br_=br_: e.matmul(
                    pb[br_][:], onesC[32:33, :], arena[32:33, tb * TBW:(tb + 1) * TBW],
                    start=True, stop=True))
                for dc in range(NDC):
                    i = tb * NDC + dc
                    sl = i % 4
                    xsl = dc % 2
                    w = [slot_use[sl]] if slot_use[sl] else []
                    ld = P.dma("sync", lambda e, dc=dc, tb=tb, sl=sl: e.dma_start(
                        out=xst2[:, sl * 512:(sl + 1) * 512],
                        in_=x1d[dc, tb]), qX[sl], w)
                    P.rd("vector", bm, lambda e, sl=sl, bm=bm, xsl=xsl: e.tensor_tensor(
                        scr2[:, xsl * 512:(xsl + 1) * 512],
                        xst2[:, sl * 512:(sl + 1) * 512], pb[bm][:], ALU.subtract), [ld])
                    wg = [lg_guard[xsl]] if lg_guard[xsl] else []
                    mtok = P.rd("vector", br_, lambda e, xsl=xsl, br_=br_: e.tensor_tensor(
                        xn2fs[:, xsl * 512:(xsl + 1) * 512],
                        scr2[:, xsl * 512:(xsl + 1) * 512], pb[br_][:], ALU.mult), wg)
                    slot_use[sl] = mtok
                    ct = P.s(lambda e, xsl=xsl: e.copy(
                        xn2bst[:, xsl * 512:(xsl + 1) * 512],
                        xn2fs[:, xsl * 512:(xsl + 1) * 512]),
                        [mtok] + ([nstore_use[xsl]] if nstore_use[xsl] else []))
                    nstore_use[xsl] = P.dma("sync", lambda e, dc=dc, tb=tb, xsl=xsl: e.dma_start(
                        out=xn2d[dc, tb],
                        in_=xn2bst[:, xsl * 512:(xsl + 1) * 512]), qN[xsl], [ct])
                    lg_guard[xsl] = P.mm(bl, lambda e, dc=dc, xsl=xsl, bl=bl: e.matmul(
                        pb[bl][0:8, :], gsb[:, dc * 8:(dc + 1) * 8],
                        xn2fs[:, xsl * 512:(xsl + 1) * 512],
                        start=(dc == 0), stop=(dc == NDC - 1)), [mtok, (qC, qC.n)])
                P.rd("vector", bl, lambda e, tb=tb, bl=bl: e.tensor_copy(
                    lg_sb[0:8, tb * TBW:(tb + 1) * TBW], pb[bl][0:8, :]))
            P.barrier_all()

            P.v(lambda e: e.memset(zz[:], 0.0))
            P.v(lambda e: e.memset(rt[:], 0.0))
            for tt in range(16):
                P.mm(6, lambda e, tt=tt: e.transpose(
                    pb[6][:, 0:8], lg_sb[0:8, tt * 128:(tt + 1) * 128], identf[0:8, 0:8]))
                P.rd("vector", 6, lambda e: e.tensor_copy(lgt[:, 0:8], pb[6][:, 0:8]))
                P.v(lambda e: e.tensor_reduce(rt[:, 64:65], lgt[:, 0:8], AX.X, ALU.max))
                P.v(lambda e: e.tensor_scalar(rt[:, 0:8], lgt[:, 0:8], rt[:, 64:65], None, ALU.is_ge))
                P.v(lambda e: e.tensor_tensor_scan(rt[:, 8:16], rt[:, 0:8], rt[:, 80:88],
                                                   0.0, ALU.add, ALU.add))
                P.v(lambda e: e.tensor_scalar(rt[:, 16:24], rt[:, 8:16], 1.0, None, ALU.is_le))
                P.v(lambda e: e.tensor_tensor(rt[:, 24:32], rt[:, 0:8], rt[:, 16:24], ALU.mult))
                P.v(lambda e: e.scalar_tensor_tensor(rt[:, 32:40], rt[:, 24:32], -1e30,
                                                     lgt[:, 0:8], ALU.mult, ALU.add))
                P.v(lambda e: e.tensor_reduce(rt[:, 65:66], rt[:, 32:40], AX.X, ALU.max))
                P.v(lambda e: e.tensor_tensor(rt[:, 66:67], rt[:, 64:65], rt[:, 65:66], ALU.subtract))
                P.s(lambda e: e.activation(rt[:, 67:68], rt[:, 66:67], AF.Sigmoid), [(P.veS, P.veS.n)])
                sg = P.s(lambda e: e.activation(rt[:, 68:69], rt[:, 66:67], AF.Sigmoid, scale=-1.0))
                P.v(lambda e: e.tensor_scalar(rt[:, 40:48], rt[:, 32:40], rt[:, 65:66], None, ALU.is_ge))
                P.v(lambda e: e.tensor_tensor_scan(rt[:, 48:56], rt[:, 40:48], rt[:, 80:88],
                                                   0.0, ALU.add, ALU.add))
                P.v(lambda e: e.tensor_scalar(rt[:, 16:24], rt[:, 48:56], 1.0, None, ALU.is_le))
                P.v(lambda e: e.tensor_tensor(rt[:, 56:64], rt[:, 40:48], rt[:, 16:24], ALU.mult))
                P.v(lambda e: e.tensor_scalar(rt[:, 0:8], rt[:, 24:32], rt[:, 67:68], None, ALU.mult), [sg])
                P.v(lambda e: e.tensor_scalar(rt[:, 8:16], rt[:, 56:64], rt[:, 68:69], None, ALU.mult))
                P.mm(6, lambda e: e.transpose(pb[6][0:8, 0:128], rt[:, 0:8], identf[:]),
                     [(P.veS, P.veS.n)])
                P.rd("vector", 6, lambda e, tt=tt: e.tensor_copy(
                    mp1[0:8, tt * 128:(tt + 1) * 128], pb[6][0:8, 0:128]))
                P.mm(7, lambda e: e.transpose(pb[7][0:8, 0:128], rt[:, 8:16], identf[:]),
                     [(P.veS, P.veS.n)])
                P.rd("vector", 7, lambda e, tt=tt: e.tensor_copy(
                    mp2[0:8, tt * 128:(tt + 1) * 128], pb[7][0:8, 0:128]))

            P.v(lambda e: e.tensor_scalar(tpl[:], mp1[:], 0.0, None, ALU.is_gt))
            P.v(lambda e: e.tensor_tensor_scan(lc1[:], tpl[:], zz[:], 0.0, ALU.add, ALU.add))
            P.v(lambda e: e.tensor_scalar(tpl[:], mp2[:], 0.0, None, ALU.is_gt))
            P.v(lambda e: e.tensor_tensor_scan(lc2[:], tpl[:], zz[:], 0.0, ALU.add, ALU.add))
            P.v(lambda e: e.tensor_scalar(cnt[:, 0:8], cmsb[:, 0:8], lc1[:, 2047:2048], None, ALU.mult))
            ctok = P.v(lambda e: e.tensor_scalar(cnt[:, 8:16], cmsb[:, 0:8], lc2[:, 2047:2048],
                                                 None, ALU.mult))
            P.dma("gpsimd", lambda e: e.dma_start(out=ccin[0:8, :], in_=cnt[:, 0:8]), qCC, [ctok])
            P.dma("gpsimd", lambda e: e.dma_start(out=ccin[8:16, :], in_=cnt[:, 8:16]), qCC)
            P.g(lambda e: e.collective_compute(
                "AllReduce", ALU.add, replica_groups=[list(range(NCORE))],
                ins=[ccin.opt()], outs=[ccout.opt()]), [(qCC, qCC.n), (ccS, 1)], [(ccS, 1)])
            l1 = P.dma("gpsimd", lambda e: e.dma_start(out=cnt[:, 16:24], in_=ccout[0:8, :]), qCC, [(ccS, 2)])
            l2 = P.dma("gpsimd", lambda e: e.dma_start(out=cnt[:, 24:32], in_=ccout[8:16, :]), qCC)
            P.v(lambda e: e.tensor_tensor(rt[0:8, 0:8], cnt[:, 16:24], cmsb[:, 8:16], ALU.mult), [l1, l2])
            P.v(lambda e: e.tensor_reduce(rt[0:8, 64:65], rt[0:8, 0:8], AX.X, ALU.add))
            P.v(lambda e: e.tensor_reduce(rt[0:8, 65:66], cnt[:, 16:24], AX.X, ALU.add))
            P.v(lambda e: e.tensor_tensor(rt[0:8, 8:16], cnt[:, 24:32], cmsb[:, 8:16], ALU.mult))
            P.v(lambda e: e.tensor_reduce(rt[0:8, 66:67], rt[0:8, 8:16], AX.X, ALU.add))
            P.v(lambda e: e.tensor_tensor(rt[0:8, 66:67], rt[0:8, 66:67], rt[0:8, 65:66], ALU.add))
            P.v(lambda e: e.tensor_scalar(tpl[:], lc1[:], rt[0:8, 64:65], None, ALU.add))
            P.v(lambda e: e.tensor_scalar(lc1[:], tpl[:], CAP, None, ALU.is_le))
            P.v(lambda e: e.tensor_tensor(mp1[:], lc1[:], mp1[:], ALU.mult))
            P.v(lambda e: e.tensor_scalar(tpl[:], lc2[:], rt[0:8, 66:67], None, ALU.add))
            P.v(lambda e: e.tensor_scalar(lc2[:], tpl[:], CAP, None, ALU.is_le))
            P.v(lambda e: e.tensor_tensor(mp2[:], lc2[:], mp2[:], ALU.mult))
            P.v(lambda e: e.tensor_tensor(w_sb[0:8, :], mp1[:], mp2[:], ALU.add))
            P.barrier_all(exclude=(qFW, qFW2))
            P.flush()
            s2.close()

            s3 = ExitStack()
            EH = s3.enter_context
            w2s = EH(nc.sbuf_tensor("w2s", [128, NHC * 1024], BF16))
            hbf = EH(nc.sbuf_tensor("hbf", [128, NHC * 512], BF16))
            xts = EH(nc.sbuf_tensor("xts", [128, 2 * NDC * 512], BF16))
            wbs = EH(nc.sbuf_tensor("wbs", [128, 2 * 512], F32))
            yst = EH(nc.sbuf_tensor("yst", [128, 2 * 512], F32))

            xts_use = [None, None]
            yst_use = [None, None]
            wbs_use = [None, None]
            p1_done = None
            p2_done = None
            hbf_guard = None
            w1_next = w1_rdy0

            def load_w2(ex, waits):
                tok = None
                for hc in range(NHC):
                    tok = P.dma("gpsimd", lambda e, ex=ex, hc=hc: e.dma_start(
                        out=w2s[:, hc * 1024:(hc + 1) * 1024],
                        in_=cpj[ex, hc * 128:(hc + 1) * 128, :]), qFW2,
                        waits if hc == 0 else [])
                return tok

            def load_w1(ex, waits):
                tok = None
                for dc in range(NDC):
                    tok = P.dma("gpsimd", lambda e, ex=ex, dc=dc: e.dma_start(
                        out=w1s[:, dc * HID:(dc + 1) * HID],
                        in_=cfc[ex, dc * 128:(dc + 1) * 128, :]), qFW,
                        waits if dc == 0 else [])
                return tok

            for ex in range(NEXP):
                w1_rdy = w1_next
                w2_rdy = load_w2(ex, [p2_done] if p2_done else [])
                for ts in range(NTB):
                    sl = (ex * NTB + ts) % 2
                    ldx = None
                    for dc in range(NDC):
                        ldx = P.dma("sync", lambda e, ts=ts, dc=dc, sl=sl: e.dma_start(
                            out=xts[:, sl * 4096 + dc * 512: sl * 4096 + (dc + 1) * 512],
                            in_=xn2d[dc, ts]), qFX[sl],
                            ([xts_use[sl]] if xts_use[sl] and dc == 0 else []))
                    wb = P.mm(4 + sl, lambda e, ex=ex, ts=ts, sl=sl: e.matmul(
                        pb[4 + sl][:], selS[:, ex * 128:(ex + 1) * 128],
                        w_sb[:, ts * TBW:(ts + 1) * TBW], start=True, stop=True),
                        [wbs_use[sl]] if wbs_use[sl] else [])
                    wbr = P.rd("scalar", 4 + sl, lambda e, sl=sl: e.copy(
                        wbs[:, sl * 512:(sl + 1) * 512], pb[4 + sl][:]))
                    for hc in range(NHC):
                        bk = hc % 2
                        def mm1(e, hc=hc, sl=sl, bk=bk):
                            last = None
                            for dc in range(NDC):
                                last = e.matmul(
                                    pb[bk][:],
                                    w1s[:, dc * HID + hc * 128: dc * HID + (hc + 1) * 128],
                                    xts[:, sl * 4096 + dc * 512: sl * 4096 + (dc + 1) * 512],
                                    start=(dc == 0), stop=(dc == NDC - 1))
                            return last
                        P.mm(bk, mm1, [w1_rdy, ldx])
                        P.rd("scalar", bk, lambda e, ex=ex, hc=hc, bk=bk: e.activation(
                            hbf[:, hc * 512:(hc + 1) * 512], pb[bk][:], AF.Gelu,
                            bias=fcbs[:, ex * 32 + hc: ex * 32 + hc + 1]),
                            [hbf_guard] if hbf_guard else [])
                    gelu_all = (P.scS, P.scS.n)
                    if ts == NTB - 1:
                        p1_done = (P.mmS, P.mmS.n)
                        if ex + 1 < NEXP:
                            w1_next = load_w1(ex + 1, [p1_done])
                    for dc in range(NDC):
                        bk = 2 + dc % 2
                        def mm2(e, dc=dc, bk=bk):
                            last = None
                            for hc in range(NHC):
                                last = e.matmul(
                                    pb[bk][:],
                                    w2s[:, hc * 1024 + dc * 128: hc * 1024 + (dc + 1) * 128],
                                    hbf[:, hc * 512:(hc + 1) * 512],
                                    start=(hc == 0), stop=(hc == NHC - 1))
                            return last
                        P.mm(bk, mm2, [gelu_all, w2_rdy])
                        ysl = dc % 2
                        ev = P.rd("vector", bk, lambda e, ex=ex, dc=dc, ysl=ysl, sl=sl, bk=bk: e.scalar_tensor_tensor(
                            yst[:, ysl * 512:(ysl + 1) * 512], pb[bk][:],
                            pjbs[:, ex * 8 + dc: ex * 8 + dc + 1],
                            wbs[:, sl * 512:(sl + 1) * 512], ALU.add, ALU.mult),
                            [wbr] + ([yst_use[ysl]] if yst_use[ysl] else []))
                        yst_use[ysl] = P.dma("gpsimd", lambda e, ex=ex, dc=dc, ts=ts, ysl=ysl: e.dma_start(
                            out=ysfm[dc, ts],
                            in_=yst[:, ysl * 512:(ysl + 1) * 512],
                            accum_op=ALU.add), qYP[ysl], [ev])
                    hbf_guard = (P.mmS, P.mmS.n)
                    xts_use[sl] = (P.mmS, P.mmS.n)
                    wbs_use[sl] = (P.veS, P.veS.n)
                p2_done = (P.mmS, P.mmS.n)
            P.barrier_all()
            s3.close()
            P.flush()

        P.emit("gpsimd", None, waits=[(qYP[0], qYP[0].n), (qYP[1], qYP[1].n),
                                      (qST[0], qST[0].n), (qST[1], qST[1].n)])
        P.flush()

    return nc


def kernel(L0_attn_x, L0_attn_Wq, L0_attn_bq, L0_attn_Wk, L0_attn_bk,
           L0_attn_Wv, L0_attn_bv, L0_attn_Wo, L0_attn_bo, L0_attn_P,
           L0_ffn_gate_w, L0_ffn_c_fc, L0_ffn_fc_bias, L0_ffn_c_proj,
           L0_ffn_proj_bias):
    x = np.ascontiguousarray(np.asarray(L0_attn_x, dtype=np.float32))
    B, N, D_ = x.shape
    X2 = x.reshape(B * N, D_)

    c = np.ascontiguousarray
    shared = dict(
        wqT=c(np.asarray(L0_attn_Wq, np.float32).T),
        wkT=c(np.asarray(L0_attn_Wk, np.float32).T),
        wvT=c(np.asarray(L0_attn_Wv, np.float32).T),
        woT=c(np.asarray(L0_attn_Wo, np.float32).T),
        bqkvo=c(np.concatenate(
            [np.asarray(b, np.float32).reshape(8, 128).T
             for b in (L0_attn_bq, L0_attn_bk, L0_attn_bv, L0_attn_bo)], axis=1)),
        bvrow=c(np.asarray(L0_attn_bv, np.float32).reshape(1, D_)),
        ptT=c(np.asarray(L0_attn_P, np.float32).T),
        gw=c(np.asarray(L0_ffn_gate_w, np.float32)),
        cfc=c(np.asarray(L0_ffn_c_fc, np.float32)),
        fcbi=c(np.stack([np.asarray(L0_ffn_fc_bias, np.float32)[e].reshape(32, 128).T
                         for e in range(NEXP)])),
        cpj=c(np.asarray(L0_ffn_c_proj, np.float32)),
        pjbi=c(np.stack([np.asarray(L0_ffn_proj_bias, np.float32)[e].reshape(8, 128).T
                         for e in range(NEXP)])),
    )

    in_maps = []
    for core in range(NCORE):
        sl = X2[core * TPC:(core + 1) * TPC]
        xfm_ = c(sl.T.reshape(NDC, 128, NTB, TBW).transpose(0, 2, 1, 3))
        cms_ = np.zeros((8, 8), np.float32)
        cms_[:, core] = 1.0
        cmp_ = np.zeros((8, 8), np.float32)
        cmp_[:, :core] = 1.0
        m = dict(shared)
        m.update(xfm=xfm_, cmself=cms_, cmprev=cmp_)
        in_maps.append(m)

    nc = build_program()
    res = run_bass_kernel_spmd(nc, in_maps, core_ids=list(range(NCORE)))

    out = np.empty((B * N, D_), np.float32)
    for core in range(NCORE):
        yf = np.asarray(res.results[core]["ysfm"])
        out[core * TPC:(core + 1) * TPC] = yf.transpose(0, 2, 1, 3).reshape(D_, TPC).T
    return out.reshape(B, N, D_)
